# version 09
# baseline (speedup 1.0000x reference)
import contextlib
import numpy as np
import ml_dtypes
import concourse.bass as bass
import concourse.mybir as mybir
from concourse.bass_utils import run_bass_kernel_spmd

F32 = mybir.dt.float32
BF16 = mybir.dt.bfloat16
AF = mybir.ActivationFunctionType
ALU = mybir.AluOpType
AX = mybir.AxisListType

NCORES = 8
D = 2048
DF = 8192
NH = 16
TP = 512
NKC = D // 128
NHC = DF // 128
EPS_L2 = 1.0e-12
SCALE = float(128) ** 0.5
E2_L2 = (SCALE * EPS_L2) ** 2
LN_EPS = 1.0e-5
Q_EPS = 1.0e-5
LADDER = [128.0, 64.0, 32.0, 16.0, 8.0, 4.0, 2.0, 1.0]
ENGINES = ("sync", "vector", "scalar", "tensor", "gpsimd")
DEBUG = ()
SAFE_QK = False
ACT_FUDGE = 0


class Builder:
    def __init__(self):
        self.prog = {e: [] for e in ENGINES}
        self.cnt = {}
        self.last_ev = {}

    def _ensure(self, name):
        if name not in self.cnt:
            self.cnt[name] = 0

    def wait(self, eng, ev):
        if ev is None:
            return
        name, val = ev
        if val <= 0:
            return
        self._ensure(name)
        self.prog[eng].append(("w", name, val))

    def emit(self, eng, fn, inc=None, k=1, drain=True):
        if inc is None:
            self.prog[eng].append(("d", fn))
            return None
        self._ensure(inc)
        self.cnt[inc] += k
        if eng in ("vector", "scalar") and drain:
            self.prog[eng].append(("d", fn))
            self.prog[eng].append(("i", lambda e: e.drain(), inc, k))
        else:
            self.prog[eng].append(("i", fn, inc, k))
        if not inc.endswith("p"):
            self.last_ev[eng] = (inc, self.cnt[inc])
        return (inc, self.cnt[inc])

    def now(self, name):
        self._ensure(name)
        return (name, self.cnt[name])

    def barrier(self, engines=ENGINES):
        evs = list(self.last_ev.values())
        for eng in engines:
            for ev in evs:
                self.wait(eng, ev)

    def replay(self, block, sems):
        def make(eng):
            def body(e):
                for it in self.prog[eng]:
                    if it[0] == "w":
                        e.wait_ge(sems[it[1]], it[2])
                    elif it[0] == "d":
                        it[1](e)
                    else:
                        it[1](e).then_inc(sems[it[2]], it[3])
            return body
        for eng in ENGINES:
            getattr(block, eng)(make(eng))


def build_program(tpc):
    npass = tpc // TP
    nc = bass.Bass("TRN2", target_bir_lowering=False)

    xT = nc.dram_tensor("xT", [D, tpc], F32, kind="ExternalInput").ap()
    wqT = nc.dram_tensor("wqT", [D, D], F32, kind="ExternalInput").ap()
    wkT = nc.dram_tensor("wkT", [D, D], F32, kind="ExternalInput").ap()
    wvT = nc.dram_tensor("wvT", [D, D], F32, kind="ExternalInput").ap()
    wogT = nc.dram_tensor("wogT", [D, D], F32, kind="ExternalInput").ap()
    woT = nc.dram_tensor("woT", [D, D], F32, kind="ExternalInput").ap()
    w1T = nc.dram_tensor("w1T", [D, DF], F32, kind="ExternalInput").ap()
    w2T = nc.dram_tensor("w2T", [DF, D], F32, kind="ExternalInput").ap()
    bwT = nc.dram_tensor("bwT", [D, NH], F32, kind="ExternalInput").ap()
    qb = nc.dram_tensor("qb", [1, D], F32, kind="ExternalInput").ap()
    kb = nc.dram_tensor("kb", [1, D], F32, kind="ExternalInput").ap()
    vb = nc.dram_tensor("vb", [1, D], F32, kind="ExternalInput").ap()
    ob = nc.dram_tensor("ob", [1, D], F32, kind="ExternalInput").ap()
    bb = nc.dram_tensor("bb", [1, NH], F32, kind="ExternalInput").ap()
    idb = nc.dram_tensor("idb", [128, 128], BF16, kind="ExternalInput").ap()
    idf = nc.dram_tensor("idf", [128, 128], F32, kind="ExternalInput").ap()
    maskf = nc.dram_tensor("maskf", [128, 128], F32, kind="ExternalInput").ap()
    onesb = nc.dram_tensor("onesb", [128, 512], BF16, kind="ExternalInput").ap()
    onesf = nc.dram_tensor("onesf", [8, 128], F32, kind="ExternalInput").ap()
    onecf = nc.dram_tensor("onecf", [128, 1], F32, kind="ExternalInput").ap()
    outT = nc.dram_tensor("outT", [D, tpc], F32, kind="ExternalOutput").ap()
    wq1S = nc.dram_tensor("wq1S", [D, DF], BF16, kind="Internal").ap()
    wq2S = nc.dram_tensor("wq2S", [DF, D], BF16, kind="Internal").ap()

    B = Builder()

    _st = contextlib.ExitStack()
    with _st:
        X = _st.enter_context(nc.sbuf_tensor("Xs", [128, NKC * TP], F32))
        BIGA = _st.enter_context(nc.sbuf_tensor("BIGAs", [128, 40960], BF16))
        WB = _st.enter_context(nc.sbuf_tensor("WBs", [128, 16384], BF16))
        WF = _st.enter_context(nc.sbuf_tensor("WFs", [128, 2048], F32))
        BETA = _st.enter_context(nc.sbuf_tensor("BETAs", [128, 64], F32))
        UT = _st.enter_context(nc.sbuf_tensor("UTs", [128, 512], F32))
        VT = _st.enter_context(nc.sbuf_tensor("VTs", [128, 512], F32))
        GT = _st.enter_context(nc.sbuf_tensor("GTs", [128, 512], F32))
        SBC = _st.enter_context(nc.sbuf_tensor("SBCs", [128, 512], F32))
        IVBC = _st.enter_context(nc.sbuf_tensor("IVBCs", [128, 512], F32))
        QTMB = _st.enter_context(nc.sbuf_tensor("QTMBs", [128, 512], BF16))
        RB = _st.enter_context(nc.sbuf_tensor("RBs", [128, 512], BF16))
        SQB = _st.enter_context(nc.sbuf_tensor("SQBs", [128, 512], BF16))
        CMHB = _st.enter_context(nc.sbuf_tensor("CMHBs", [128, 512], BF16))
        MSA = _st.enter_context(nc.sbuf_tensor("MSAs", [128, 128], BF16))
        MSB = _st.enter_context(nc.sbuf_tensor("MSBs", [128, 128], BF16))
        SMALL = _st.enter_context(nc.sbuf_tensor("SMALLs", [128, 64], F32))
        RMSC = _st.enter_context(nc.sbuf_tensor("RMSCs", [128, 8], F32))
        T8 = _st.enter_context(nc.sbuf_tensor("T8s", [8, 128], F32))
        UT2 = _st.enter_context(nc.sbuf_tensor("UT2s", [128, 512], F32))
        VT2 = _st.enter_context(nc.sbuf_tensor("VT2s", [128, 512], F32))
        GT2 = _st.enter_context(nc.sbuf_tensor("GT2s", [128, 512], F32))
        CMHB2 = _st.enter_context(nc.sbuf_tensor("CMHB2s", [128, 512], BF16))
        SQF2 = _st.enter_context(nc.sbuf_tensor("SQF2s", [128, 1024], F32))
        NEGH = _st.enter_context(nc.sbuf_tensor("NEGHs", [128, 4], F32))
        HALFP = _st.enter_context(nc.sbuf_tensor("HALFPs", [128, 1], F32))
        STR = _st.enter_context(nc.sbuf_tensor("STRs", [1, 3072], F32))
        SM2 = _st.enter_context(nc.sbuf_tensor("SM2s", [1, 64], F32))
        IDB = _st.enter_context(nc.sbuf_tensor("IDBs", [128, 128], BF16))
        IDF = _st.enter_context(nc.sbuf_tensor("IDFs", [128, 128], F32))
        MASK = _st.enter_context(nc.sbuf_tensor("MASKs", [128, 128], F32))
        ONESB = _st.enter_context(nc.sbuf_tensor("ONESBs", [128, 512], BF16))
        ONESF = _st.enter_context(nc.sbuf_tensor("ONESFs", [8, 128], F32))
        ONECF = _st.enter_context(nc.sbuf_tensor("ONECFs", [128, 1], F32))
        SM3 = _st.enter_context(nc.sbuf_tensor("SM3s", [1, 16], F32))
        HALFC = _st.enter_context(nc.sbuf_tensor("HALFCs", [1, 1], F32))
        BWTB = _st.enter_context(nc.sbuf_tensor("BWTBs", [128, 256], BF16))
        BROW = _st.enter_context(nc.sbuf_tensor("BROWs", [1, D], BF16))
        BBB = _st.enter_context(nc.sbuf_tensor("BBBs", [1, NH], BF16))
        PS0 = _st.enter_context(nc.psum_tensor("PS0", [128, 512], F32))
        PS1 = _st.enter_context(nc.psum_tensor("PS1", [128, 512], F32))
        PS2 = _st.enter_context(nc.psum_tensor("PS2", [128, 512], F32))
        PS3 = _st.enter_context(nc.psum_tensor("PS3", [128, 512], F32))
        PSSUM = _st.enter_context(nc.psum_tensor("PSSUM", [1, 512], F32))
        PSSQ = _st.enter_context(nc.psum_tensor("PSSQ", [1, 512], F32))
        PSTRB = _st.enter_context(nc.psum_tensor("PSTRB", [128, 512], BF16))
        PSTRF = _st.enter_context(nc.psum_tensor("PSTRF", [128, 512], F32))
        PSB = [PS0, PS1, PS2, PS3]
        CM = GT
        SQS = UT

        QNr = BIGA[:, 0:8192].rearrange("p (h n) -> p h n", n=512)
        KNr = BIGA[:, 8192:16384].rearrange("p (h n) -> p h n", n=512)

        def QN(h):
            return BIGA[:, 512 * h:512 * (h + 1)]

        def KN(h):
            return BIGA[:, 8192 + 512 * h:8192 + 512 * (h + 1)]

        def GVT(tt):
            return BIGA[:, 16384 + 2048 * tt:16384 + 2048 * (tt + 1)]

        def USL(h):
            return BIGA[:, 24576 + 512 * h:24576 + 512 * (h + 1)]

        def XBc(kc):
            return BIGA[:, 32768 + 512 * kc:32768 + 512 * (kc + 1)]

        XBfull = BIGA[:, 32768:40960]
        X2Qc = XBc

        def Hc(hc):
            return BIGA[:, 512 * hc:512 * (hc + 1)]

        def Xc(kc):
            return X[:, 512 * kc:512 * (kc + 1)]

        def WSLOT(w):
            return WB[:, 8192 * w:8192 * (w + 1)]

        def Wtile(w, kc):
            return WB[:, 8192 * w + 512 * kc:8192 * w + 512 * (kc + 1)]

        def STRr(r):
            return STR[0:1, 512 * r:512 * (r + 1)]

        psum_free = {i: None for i in range(4)}
        strips_free = [None]
        trb_free = [None]
        trf_free = [None]
        wslot_free = {0: [], 1: []}
        scratch_ev = {0: None, 1: None}
        wslot_ready = {0: None, 1: None}
        wslot_next = [0]
        wf_free = [None]
        wf2_free = [None]
        sqs_free = [None]
        qtmb_free = [None]
        rbsq_free = [None]
        ms_free = {0: None, 1: None}
        bank_i = [0]
        rv_hist = [None, None]
        brow_free = [None]

        def next_bank2():
            b = bank_i[0] % 4
            bank_i[0] += 1
            return b

        def remit2(fn, drain=False):
            B.wait("vector", rv_hist[0])
            ev = B.emit("vector", fn, inc="sV", drain=drain)
            rv_hist[0], rv_hist[1] = rv_hist[1], ev
            return ev

        def waitall(eng, evs):
            for ev in evs:
                B.wait(eng, ev)

        gdma_last = {}

        def dma(eng, out_ap, in_ap, inc="sD"):
            if eng == "gpsimd":
                inc = inc + "g"
                B.wait("gpsimd", gdma_last.get(inc))
            ev = B.emit(eng, lambda e, o=out_ap, i=in_ap: e.dma_start(out=o, in_=i), inc=inc, k=16)
            if eng == "gpsimd":
                gdma_last[inc] = ev
            return ev

        def cs(wt, c0, cw, k0=None, kn=None):
            r = wt.rearrange("(k p) n -> p k n", p=128)
            if k0 is None:
                return r[:, :, c0:c0 + cw]
            return r[:, k0:k0 + kn, c0:c0 + cw]

        def load_wslice(in_ap, cast, colw=512):
            w = wslot_next[0]
            wslot_next[0] = 1 - w
            eng = "gpsimd" if cast else "sync"
            waitall(eng, wslot_free[w])
            ev = dma(eng, WSLOT(w).rearrange("p (k n) -> p k n", n=colw), in_ap, inc="sW" + str(w))
            wslot_ready[w] = ev
            return w

        def _dbg(name, ap):
            if name not in DEBUG:
                return
            sh = list(ap.shape)
            t = nc.dram_tensor("dbg_" + name, sh, ap.dtype, kind="ExternalOutput").ap()
            B.barrier()
            idx = tuple(slice(0, s) for s in sh)
            ev = dma("sync", t[idx], ap, inc="sO")
            for eng in ENGINES:
                B.wait(eng, ev)

        dma("sync", IDB[:], idb[:, :], inc="sC")
        dma("sync", IDF[:], idf[:, :], inc="sC")
        dma("sync", MASK[:], maskf[:, :], inc="sC")
        dma("sync", ONESB[:], onesb[:, :], inc="sC")
        dma("sync", ONESF[:], onesf[:, :], inc="sC")
        dma("sync", ONECF[:], onecf[:, :], inc="sC")
        dma("gpsimd", BBB[:], bb[:, :], inc="sC")
        dma("gpsimd", BWTB[:].rearrange("p (k n) -> p k n", n=NH),
            bwT.rearrange("(k p) n -> p k n", p=128), inc="sC")
        B.emit("gpsimd", lambda e: e.memset(NEGH[:], -0.5), inc="sC", k=1)
        B.emit("gpsimd", lambda e: e.memset(HALFC[:], 0.5), inc="sC", k=1)
        B.emit("gpsimd", lambda e: e.memset(HALFP[:], 0.5), inc="sC", k=1)
        for ev in (B.now("sC"), B.now("sCg")):
            for eng in ENGINES:
                B.wait(eng, ev)

        ONE_ROW_B = ONESB[0:1, :]
        ONE_COL_B = ONESB[:, 0:1]

        pp_quarters = []
        for m, nkq in ((0, 8), (1, 32)):
            for kq in range(nkq):
                for cq in range(128 // nkq):
                    pp_quarters.append((m, kq, cq))
        w_r = {0: w1T.rearrange("(k p) n -> p k n", p=128),
               1: w2T.rearrange("(k p) n -> p k n", p=128)}
        pp = {"pos": 0, "qidx": 0, "ev_mm": None}
        rmsc_ev = [None, None]

        def pp_finalize(m):
            B.emit("vector", lambda e: e.tensor_reduce(
                SM3[0:1, 8:10], SM2[0:1, 0:16].rearrange("p (s n) -> p s n", n=8), AX.X, ALU.add))
            B.emit("vector", lambda e: e.drain())
            B.emit("vector", lambda e: e.tensor_reduce(SM3[0:1, 0:1], SM3[0:1, 8:10], AX.X, ALU.add))
            B.emit("vector", lambda e: e.drain())
            ev = B.emit("vector", lambda e: e.tensor_scalar(
                SM3[0:1, 1:2], SM3[0:1, 0:1], 1.0 / float(D * DF), Q_EPS, ALU.mult, ALU.max),
                inc="sVp")
            B.wait("gpsimd", ev)
            ev = B.emit("gpsimd", lambda e: e.tensor_tensor(
                SM3[0:1, 2:3], SM3[0:1, 1:2], HALFC[:], ALU.pow), inc="sGp")
            B.wait("vector", ev)
            B.emit("vector", lambda e: e.tensor_scalar(SM3[0:1, 4:5], SM3[0:1, 2:3], 0.5, None, ALU.mult))
            B.emit("vector", lambda e: e.tensor_scalar(SM3[0:1, 5:6], SM3[0:1, 2:3], -0.5, None, ALU.mult))
            B.emit("vector", lambda e: e.tensor_scalar(SM3[0:1, 6:7], SM3[0:1, 2:3], 1.0, None, ALU.mult))
            ev = B.emit("vector", lambda e: e.tensor_scalar(
                SM3[0:1, 7:8], SM3[0:1, 2:3], -1.0, None, ALU.mult), inc="sVp")
            B.wait("tensor", ev)
            B.wait("tensor", trf_free[0])
            ev = B.emit("tensor", lambda e: e.matmul(
                PSTRF[:, 0:4], ONESF[0:1, :], SM3[0:1, 4:8], start=True, stop=True), inc="sTp")
            B.wait("vector", ev)
            ev = B.emit("vector", lambda e, m_=m: e.tensor_copy(
                RMSC[:, 4 * m_:4 * m_ + 4], PSTRF[:, 0:4]), inc="sVp")
            trf_free[0] = ev
            rmsc_ev[m] = ev

        def prepass_pump(n):
            assert n % 8 == 0
            for _ in range(n):
                if pp["pos"] >= len(pp_quarters):
                    return
                m, kq, cq = pp_quarters[pp["pos"]]
                pp["pos"] += 1
                odd = pp["qidx"] % 2
                buf = WF[:, 1024:2048] if odd else WF[:, 0:1024]
                guard = wf2_free if odd else wf_free
                B.wait("sync", guard[0])
                ev_wf = dma("sync", buf.rearrange("p (k n) -> p k n", n=512),
                            w_r[m][:, 2 * kq:2 * kq + 2, 512 * cq:512 * (cq + 1)],
                            inc="sFBp" if odd else "sFp")
                B.wait("scalar", ev_wf)
                B.wait("scalar", pp["ev_mm"])
                ev_sq = B.emit("scalar", lambda e, b_=buf: e.activation(SQF2[:], b_, AF.Square),
                               inc="sAp")
                guard[0] = ev_sq
                B.wait("tensor", ev_sq)
                grp = pp["qidx"] % 8
                if grp == 0:
                    B.wait("tensor", strips_free[0])
                ev = None
                for c4 in range(2):
                    ev = B.emit("tensor", lambda e, c_=c4, g_=grp: e.matmul(
                        PSSUM[:], ONECF[:], SQF2[:, 512 * c_:512 * (c_ + 1)],
                        start=(g_ == 0 and c_ == 0), stop=(g_ == 7 and c_ == 1)), inc="sTp")
                pp["ev_mm"] = ev
                if grp == 7:
                    B.wait("vector", ev)
                    B.emit("vector", lambda e: e.tensor_reduce(
                        SM3[0:1, 8:12], PSSUM[0:1, :].rearrange("p (s n) -> p s n", n=128),
                        AX.X, ALU.add))
                    B.emit("vector", lambda e: e.drain())
                    ev_str = B.emit("vector", lambda e, s_=(pp["qidx"] // 8) % 16: e.tensor_reduce(
                        SM2[0:1, s_:s_ + 1], SM3[0:1, 8:12], AX.X, ALU.add), inc="sVp")
                    strips_free[0] = ev_str
                pp["qidx"] += 1
                if pp["qidx"] in (128, 256):
                    pp_finalize(0 if pp["qidx"] == 128 else 1)
        hook = {"fn": None}

        def pump_hook():
            if hook["fn"]:
                hook["fn"]()

        def phase_x_load(p, prev_out_ev):
            B.wait("sync", prev_out_ev)
            ev = dma("sync", X.rearrange("p (k n) -> p k n", n=512),
                     xT.rearrange("(k p) n -> p k n", p=128)[:, :, TP * p:TP * (p + 1)], inc="sX")
            B.wait("vector", ev)
            return B.emit("vector", lambda e: e.tensor_copy(XBfull, X[:]), inc="sV")

        def phase_beta(xb_ev):
            B.wait("tensor", xb_ev)
            ev = None
            for tt in range(4):
                for kc in range(16):
                    B.emit("tensor", lambda e, tt_=tt, kc_=kc: e.matmul(
                        PS2[:, 16 * tt_:16 * (tt_ + 1)],
                        XBc(kc_)[:, 128 * tt_:128 * (tt_ + 1)],
                        BWTB[:, NH * kc_:NH * (kc_ + 1)],
                        start=(kc_ == 0), stop=False))
                ev = B.emit("tensor", lambda e, tt_=tt: e.matmul(
                    PS2[:, 16 * tt_:16 * (tt_ + 1)], ONE_ROW_B[:, 0:128], BBB[0:1, :],
                    start=False, stop=True), inc="sT")
            B.wait("scalar", ev)
            B.emit("scalar", lambda e: e.activation(SMALL[:, 0:64], PS2[:, 0:64], AF.Abs), inc="sA")
            B.emit("scalar", lambda e: e.activation(SMALL[:, 0:64], SMALL[:, 0:64], AF.Exp, scale=-1.0))
            ev = B.emit("scalar", lambda e: e.activation(BETA[:], PS2[:, 0:64], AF.Relu), inc="sA")
            psum_free[2] = ev
            B.wait("vector", ev)
            ev = B.emit("vector", lambda e: e.tensor_scalar(
                SMALL[:, 0:64], SMALL[:, 0:64], 1.0, None, ALU.add), inc="sV")
            B.wait("scalar", ev)
            ev = B.emit("scalar", lambda e: e.activation(SMALL[:, 0:64], SMALL[:, 0:64], AF.Ln), inc="sA")
            B.wait("vector", ev)
            return B.emit("vector", lambda e: e.tensor_tensor(
                BETA[:], BETA[:], SMALL[:, 0:64], ALU.add), inc="sV")

        def phase_qk(wt, bias_dram, dest_r):
            B.wait("gpsimd", brow_free[0])
            ev_b = dma("gpsimd", BROW[:], bias_dram[:, :], inc="sC")
            B.wait("tensor", ev_b)
            ev_last = None
            for s in range(4):
                pump_hook()
                w = load_wslice(cs(wt, 512 * s, 512), cast=True)
                for tt in range(4):
                    bi = next_bank2()
                    pb = PSB[bi]
                    B.wait("tensor", psum_free[bi])
                    B.wait("tensor", wslot_ready[w])
                    for kc in range(16):
                        B.emit("tensor", lambda e, kc_=kc, tt_=tt, w_=w, pb_=pb: e.matmul(
                            pb_[:], XBc(kc_)[:, 128 * tt_:128 * (tt_ + 1)], Wtile(w_, kc_),
                            start=(kc_ == 0), stop=False))
                    ev_mm = B.emit("tensor", lambda e, pb_=pb, s_=s: e.matmul(
                        pb_[:], ONE_ROW_B[:, 0:128], BROW[0:1, 512 * s_:512 * (s_ + 1)],
                        start=False, stop=True), inc="sT")
                    ev_last = ev_mm
                    if tt == 3:
                        wslot_free[w] = [ev_mm]
                    B.wait("scalar", ev_mm)
                    B.wait("scalar", sqs_free[0])
                    ev_sq = B.emit("scalar", lambda e, pb_=pb: e.activation(
                        SQS[:], pb_[:], AF.Square), inc="sA")
                    B.wait("vector", ev_sq)
                    B.emit("vector", lambda e: e.tensor_reduce(
                        SMALL[:, 0:4], SQS[:].rearrange("p (s n) -> p s n", n=128),
                        AX.X, ALU.add))
                    ev_red = B.emit("vector", lambda e: e.drain(), inc="sV")
                    sqs_free[0] = ev_red
                    ev_max = B.emit("vector", lambda e: e.tensor_scalar(
                        SMALL[:, 4:8], SMALL[:, 0:4], E2_L2, None, ALU.max), inc="sV")
                    B.wait("gpsimd", ev_max)
                    ev_r = B.emit("gpsimd", lambda e: e.tensor_tensor(
                        SMALL[:, 12:16], SMALL[:, 4:8], NEGH[:], ALU.pow), inc="sG")
                    B.wait("vector", ev_r)
                    B.wait("vector", qtmb_free[0])
                    ev_q = None
                    for j in range(4):
                        ev_q = B.emit("vector", lambda e, j_=j, pb_=pb: e.tensor_scalar(
                            QTMB[:, 128 * j_:128 * (j_ + 1)], pb_[:, 128 * j_:128 * (j_ + 1)],
                            SMALL[:, 12 + j_:13 + j_], None, ALU.mult),
                            inc="sV" if j == 3 else None)
                    psum_free[bi] = ev_q
                    B.wait("tensor", ev_q)
                    B.wait("tensor", trb_free[0])
                    ev_t = None
                    for j in range(4):
                        ev_t = B.emit("tensor", lambda e, j_=j: e.transpose(
                            PSTRB[:, 128 * j_:128 * (j_ + 1)], QTMB[:, 128 * j_:128 * (j_ + 1)],
                            IDB[:]), inc="sT")
                    qtmb_free[0] = ev_t
                    B.wait("vector", ev_t)
                    ev_e = B.emit("vector", lambda e, s_=s, tt_=tt, dr_=dest_r: e.tensor_copy(
                        dr_[:, 4 * s_:4 * s_ + 4, 128 * tt_:128 * (tt_ + 1)],
                        PSTRB[:].rearrange("p (j n) -> p j n", n=128)), inc="sV",
                        drain=(s == 3 and tt == 3))
                    trb_free[0] = ev_e
            brow_free[0] = ev_last

        def phase_v():
            B.wait("gpsimd", brow_free[0])
            ev_b = dma("gpsimd", BROW[:], vb[:, :], inc="sC")
            B.wait("tensor", ev_b)
            ev_last = None
            for s in range(4):
                pump_hook()
                w = load_wslice(cs(wvT, 512 * s, 512), cast=True)
                for tt in range(4):
                    bi = next_bank2()
                    pb = PSB[bi]
                    B.wait("tensor", psum_free[bi])
                    B.wait("tensor", wslot_ready[w])
                    for kc in range(16):
                        B.emit("tensor", lambda e, kc_=kc, tt_=tt, w_=w, pb_=pb: e.matmul(
                            pb_[:], XBc(kc_)[:, 128 * tt_:128 * (tt_ + 1)], Wtile(w_, kc_),
                            start=(kc_ == 0), stop=False))
                    ev_mm = B.emit("tensor", lambda e, pb_=pb, s_=s: e.matmul(
                        pb_[:], ONE_ROW_B[:, 0:128], BROW[0:1, 512 * s_:512 * (s_ + 1)],
                        start=False, stop=True), inc="sT")
                    ev_last = ev_mm
                    if tt == 3:
                        wslot_free[w] = [ev_mm]
                    B.wait("vector", ev_mm)
                    ev_gv = None
                    for j in range(4):
                        h = 4 * s + j
                        ev_gv = B.emit("vector", lambda e, j_=j, tt_=tt, h_=h, pb_=pb: e.tensor_scalar(
                            GVT(tt_)[:, 128 * h_:128 * (h_ + 1)],
                            pb_[:, 128 * j_:128 * (j_ + 1)],
                            BETA[:, 16 * tt_ + h_:16 * tt_ + h_ + 1], None, ALU.mult),
                            inc="sV" if j == 3 else None,
                            drain=(s == 3 and tt == 3))
                    psum_free[bi] = ev_gv
            brow_free[0] = ev_last

        def phase_og():
            for s in range(4):
                pump_hook()
                w = load_wslice(cs(wogT, 512 * s, 512), cast=True)
                for j in range(4):
                    h = 4 * s + j
                    bi = next_bank2()
                    pb = PSB[bi]
                    B.wait("tensor", psum_free[bi])
                    B.wait("tensor", wslot_ready[w])
                    ev_mm = None
                    for kc in range(16):
                        ev_mm = B.emit("tensor", lambda e, kc_=kc, j_=j, w_=w, pb_=pb: e.matmul(
                            pb_[:], Wtile(w_, kc_)[:, 128 * j_:128 * (j_ + 1)], XBc(kc_),
                            start=(kc_ == 0), stop=(kc_ == 15)),
                            inc="sT" if kc == 15 else None)
                    if j == 3:
                        wslot_free[w] = [ev_mm]
                    B.wait("scalar", ev_mm)
                    ev_si = B.emit("scalar", lambda e, h_=h, pb_=pb: e.activation(
                        USL(h_), pb_[:], AF.Silu), inc="sA")
                    psum_free[bi] = ev_si

        def phase_att():
            i = 0
            for tt in range(4):
                pump_hook()
                for h in range(NH):
                    a = i % 2
                    pb_s = PSB[a]
                    pb_av = PSB[2 + a]
                    MS = MSA if a == 0 else MSB
                    B.wait("tensor", psum_free[a])
                    ev_s = B.emit("tensor", lambda e, tt_=tt, h_=h, pb_=pb_s: e.matmul(
                        pb_[:, 0:128], KN(h_)[:, 128 * tt_:128 * (tt_ + 1)],
                        QN(h_)[:, 128 * tt_:128 * (tt_ + 1)], start=True, stop=True), inc="sT")
                    B.wait("vector", ev_s)
                    B.wait("vector", ms_free[a])
                    ev_ms = B.emit("vector", lambda e, pb_=pb_s, MS_=MS: e.tensor_tensor(
                        MS_[:], pb_[:, 0:128], MASK[:], ALU.mult), inc="sV")
                    psum_free[a] = ev_ms
                    B.wait("tensor", ev_ms)
                    B.wait("tensor", psum_free[2 + a])
                    ev_a = B.emit("tensor", lambda e, tt_=tt, h_=h, pb_=pb_av, MS_=MS: e.matmul(
                        pb_[:, 0:128], GVT(tt_)[:, 128 * h_:128 * (h_ + 1)], MS_[:],
                        start=True, stop=True), inc="sT")
                    ms_free[a] = ev_a
                    B.wait("vector", ev_a)
                    ev_u = B.emit("vector", lambda e, tt_=tt, h_=h, pb_=pb_av: e.tensor_tensor(
                        USL(h_)[:, 128 * tt_:128 * (tt_ + 1)], pb_[:, 0:128],
                        USL(h_)[:, 128 * tt_:128 * (tt_ + 1)], ALU.mult), inc="sV",
                        drain=(i == 63))
                    psum_free[2 + a] = ev_u
                    i += 1

        def phase_oproj():
            B.wait("gpsimd", brow_free[0])
            ev_b = dma("gpsimd", BROW[:], ob[:, :], inc="sC")
            B.wait("tensor", ev_b)
            ev_stats = None
            for s in range(4):
                w = load_wslice(cs(woT, 512 * s, 512), cast=True)
                for j in range(4):
                    och = 4 * s + j
                    bi = next_bank2()
                    pb = PSB[bi]
                    B.wait("tensor", psum_free[bi])
                    B.wait("tensor", wslot_ready[w])
                    for kc in range(16):
                        B.emit("tensor", lambda e, kc_=kc, j_=j, w_=w, pb_=pb: e.matmul(
                            pb_[:], Wtile(w_, kc_)[:, 128 * j_:128 * (j_ + 1)], USL(kc_),
                            start=(kc_ == 0), stop=False))
                    ev_mm = B.emit("tensor", lambda e, pb_=pb, och_=och: e.matmul(
                        pb_[:], BROW[0:1, 128 * och_:128 * (och_ + 1)], ONE_ROW_B,
                        start=False, stop=True), inc="sT")
                    if j == 3:
                        wslot_free[w] = [ev_mm]
                    B.wait("vector", ev_mm)
                    B.wait("vector", rbsq_free[0])
                    ev_r = B.emit("vector", lambda e, pb_=pb, och_=och: e.tensor_tensor(
                        Xc(och_), pb_[:], Xc(och_), ALU.add), inc="sV", drain=False)
                    psum_free[bi] = ev_r
                    B.wait("vector", ev_r)
                    B.emit("vector", lambda e, och_=och: e.tensor_copy(RB[:], Xc(och_)))
                    ev_sqb = B.emit("vector", lambda e, och_=och: e.tensor_tensor(
                        SQB[:], Xc(och_), Xc(och_), ALU.mult), inc="sV")
                    B.wait("tensor", ev_sqb)
                    B.wait("tensor", strips_free[0])
                    B.emit("tensor", lambda e, och_=och: e.matmul(
                        PSSUM[:], ONE_COL_B, RB[:], start=(och_ == 0), stop=(och_ == 15)))
                    ev_stats = B.emit("tensor", lambda e, och_=och: e.matmul(
                        PSSQ[:], ONE_COL_B, SQB[:], start=(och_ == 0), stop=(och_ == 15)), inc="sT")
                    rbsq_free[0] = ev_stats
            brow_free[0] = ev_mm
            return ev_stats

        def phase_ln(ev_stats):
            B.wait("vector", ev_stats)
            B.emit("vector", lambda e: e.tensor_scalar(
                STRr(0), PSSUM[0:1, :], 1.0 / D, None, ALU.mult))
            ev_strips = B.emit("vector", lambda e: e.tensor_scalar(
                STRr(1), PSSQ[0:1, :], 1.0 / D, None, ALU.mult), inc="sV")
            strips_free[0] = ev_strips
            B.emit("vector", lambda e: e.tensor_tensor(STRr(2), STRr(0), STRr(0), ALU.mult), inc="sV")
            B.emit("vector", lambda e: e.tensor_tensor(STRr(2), STRr(1), STRr(2), ALU.subtract), inc="sV")
            B.emit("vector", lambda e: e.tensor_scalar(STRr(3), STRr(2), LN_EPS, None, ALU.add), inc="sV")
            B.emit("vector", lambda e: e.reciprocal(STRr(4), STRr(3)))
            ev = B.emit("vector", lambda e: e.drain(), inc="sV")
            B.wait("scalar", ev)
            ev = B.emit("scalar", lambda e: e.activation(STRr(5), STRr(4), AF.Sqrt), inc="sA")
            B.wait("tensor", ev)
            B.wait("tensor", ev_strips)
            B.wait("tensor", trf_free[0])
            ev = B.emit("tensor", lambda e: e.matmul(
                PSTRF[:], ONESF[0:1, :], STRr(0), start=True, stop=True), inc="sT")
            B.wait("vector", ev)
            ev = B.emit("vector", lambda e: e.tensor_copy(SBC[:], PSTRF[:]), inc="sV")
            B.wait("tensor", ev)
            ev = B.emit("tensor", lambda e: e.matmul(
                PSTRF[:], ONESF[0:1, :], STRr(5), start=True, stop=True), inc="sT")
            B.wait("vector", ev)
            ev = B.emit("vector", lambda e: e.tensor_copy(IVBC[:], PSTRF[:]), inc="sV")
            trf_free[0] = ev
            ev_ap = None
            for kc0 in range(0, 16, 2):
                for i in range(2):
                    remit2(lambda e, kc_=kc0 + i, G_=(GT, GT2)[i]: e.tensor_tensor(
                        G_[:], Xc(kc_), SBC[:], ALU.subtract))
                for i in range(2):
                    last = kc0 + i == 15
                    ev_ap = remit2(lambda e, kc_=kc0 + i, G_=(GT, GT2)[i]: e.tensor_tensor(
                        Xc(kc_), G_[:], IVBC[:], ALU.mult), drain=last)
            return ev_ap

        def make_scale_bcasts(ev_small):
            B.wait("tensor", ev_small)
            B.wait("tensor", trf_free[0])
            for half, dstc in ((0, 4), (1, 5)):
                ev = None
                for wdx in range(4):
                    ev = B.emit("tensor", lambda e, w_=wdx, h_=half: e.transpose(
                        PSTRF[0:1, 128 * w_:128 * (w_ + 1)],
                        SMALL[:, 16 + 4 * h_ + w_:17 + 4 * h_ + w_], IDF[:]), inc="sT")
                B.wait("vector", ev)
                ev = B.emit("vector", lambda e, d_=dstc: e.tensor_copy(
                    STRr(d_), PSTRF[0:1, 0:512]), inc="sV")
                B.wait("tensor", ev)
            ev = B.emit("tensor", lambda e: e.matmul(
                PSTRF[:], ONESF[0:1, :], STRr(4), start=True, stop=True), inc="sT")
            B.wait("vector", ev)
            ev = B.emit("vector", lambda e: e.tensor_copy(SBC[:], PSTRF[:]), inc="sV")
            B.wait("tensor", ev)
            ev = B.emit("tensor", lambda e: e.matmul(
                PSTRF[:], ONESF[0:1, :], STRr(5), start=True, stop=True), inc="sT")
            B.wait("vector", ev)
            ev = B.emit("vector", lambda e: e.tensor_copy(IVBC[:], PSTRF[:]), inc="sV")
            trf_free[0] = ev

        LSET = ((UT, VT, GT), (UT2, VT2, GT2))

        lad_prev = [None, None]

        def ladder_pair(jobs, flush=False):
            n = len(jobs)
            ev_w = [None] * n
            ev_sg = [None] * n
            for i in range(n):
                B.wait("vector", lad_prev[i])
                ev_w[i] = B.emit("vector", lambda e, s_=jobs[i][0], U_=LSET[i][0]: e.tensor_tensor(
                    U_[:], s_, SBC[:], ALU.mult), inc="sV", drain=False)
            for li, c in enumerate(LADDER):
                s = 1.0 if li % 2 == 0 else -1.0
                for i in range(n):
                    wsrc = LSET[i][0] if li == 0 else LSET[i][1]
                    B.wait("scalar", ev_w[i])
                    ev_sg[i] = B.emit("scalar", lambda e, w_=wsrc, G_=LSET[i][2], s2_=s: e.activation(
                        G_[:], w_[:], AF.Sign, bias=HALFP[0:128, 0:1], scale=s2_), inc="sA")
                for i in range(n):
                    wsrc = LSET[i][0] if li == 0 else LSET[i][1]
                    B.wait("vector", ev_sg[i])
                    ev_w[i] = B.emit("vector", lambda e, w_=wsrc, V_=LSET[i][1], G_=LSET[i][2],
                                     k_=s * c / 2: e.scalar_tensor_tensor(
                        V_[:], G_[:], k_, w_[:], ALU.mult, ALU.subtract), inc="sV", drain=False)
            evt = [None] * n
            for i in range(n):
                B.wait("vector", ev_w[i])
                evt[i] = B.emit("vector", lambda e, U_=LSET[i][0], V_=LSET[i][1],
                                G_=LSET[i][2]: e.tensor_tensor(
                    G_[:], U_[:], V_[:], ALU.subtract), inc="sV", drain=False)
            ev = None
            for i in range(n):
                B.wait("vector", evt[i])
                ev = B.emit("vector", lambda e, d_=jobs[i][1], G_=LSET[i][2]: e.scalar_tensor_tensor(
                    d_, G_[:], -0.5, IVBC[:], ALU.add, ALU.mult), inc="sV",
                    drain=(flush and i == n - 1))
                lad_prev[i] = ev
            return ev

        def scale_math():
            B.emit("vector", lambda e: e.tensor_scalar(SMALL[:, 4:8], SMALL[:, 0:4], Q_EPS, None, ALU.max))
            B.emit("vector", lambda e: e.drain())
            B.emit("vector", lambda e: e.reciprocal(SMALL[:, 8:12], SMALL[:, 4:8]))
            B.emit("vector", lambda e: e.drain())
            B.emit("vector", lambda e: e.tensor_scalar(SMALL[:, 16:20], SMALL[:, 8:12], 127.0, None, ALU.mult))
            return B.emit("vector", lambda e: e.tensor_scalar(
                SMALL[:, 20:24], SMALL[:, 4:8], 1.0 / 127.0, None, ALU.mult), inc="sV")

        def phase_qx():
            remit2(lambda e: e.tensor_tensor(CM[:], Xc(0), Xc(0), ALU.max))
            remit2(lambda e: e.tensor_scalar(SBC[:], Xc(0), -1.0, None, ALU.mult))
            for kc in range(1, 16):
                remit2(lambda e, kc_=kc: e.tensor_tensor(CM[:], Xc(kc_), CM[:], ALU.max))
                evb = remit2(lambda e, kc_=kc: e.scalar_tensor_tensor(
                    SBC[:], Xc(kc_), -1.0, SBC[:], ALU.mult, ALU.max))
            B.wait("vector", evb)
            ev_cm = B.emit("vector", lambda e: e.tensor_tensor(
                CM[:], CM[:], SBC[:], ALU.max), inc="sV")
            B.wait("tensor", ev_cm)
            B.wait("tensor", trf_free[0])
            ev_t = None
            for wdx in range(4):
                ev_t = B.emit("tensor", lambda e, w_=wdx: e.transpose(
                    PSTRF[:, 128 * w_:128 * (w_ + 1)], CM[:, 128 * w_:128 * (w_ + 1)],
                    IDF[:]), inc="sT")
            B.wait("vector", ev_t)
            for wdx in range(4):
                B.emit("vector", lambda e, w_=wdx: e.tensor_reduce(
                    SMALL[:, w_:w_ + 1], PSTRF[:, 128 * w_:128 * (w_ + 1)], AX.X, ALU.max))
            ev = B.emit("vector", lambda e: e.drain(), inc="sV")
            trf_free[0] = ev
            ev_sm = scale_math()
            make_scale_bcasts(ev_sm)
            ev = None
            for kc in range(0, 16, 2):
                ev = ladder_pair([(Xc(kc), X2Qc(kc)), (Xc(kc + 1), X2Qc(kc + 1))],
                                 flush=(kc == 14))
            return ev

        def wq_halves(wt_r, c0, k0_base, scratch_r, rms_off, w):
            B.wait("vector", rmsc_ev[rms_off // 4])
            ev_q = None
            for qtr in range(4):
                B.wait("sync", wf_free[0])
                B.wait("sync", wf2_free[0])
                ev_wf = dma("sync", WF[:].rearrange("p (k n) -> p k n", n=512),
                            wt_r[:, k0_base + 4 * qtr:k0_base + 4 * qtr + 4, c0:c0 + 512],
                            inc="sF")
                B.wait("vector", ev_wf)
                if qtr == 0:
                    waitall("vector", wslot_free[w])
                for cp in (0, 2):
                    for i in range(2):
                        remit2(lambda e, c_=cp + i, r_=rms_off, U_=LSET[i][0]: e.tensor_scalar(
                            U_[:], WF[:, 512 * c_:512 * (c_ + 1)],
                            RMSC[:, r_:r_ + 1], RMSC[:, r_ + 2:r_ + 3], ALU.is_ge, ALU.mult))
                    for i in range(2):
                        remit2(lambda e, c_=cp + i, r_=rms_off, V_=LSET[i][1]: e.tensor_scalar(
                            V_[:], WF[:, 512 * c_:512 * (c_ + 1)],
                            RMSC[:, r_ + 1:r_ + 2], RMSC[:, r_ + 3:r_ + 4], ALU.is_le, ALU.mult))
                    for i in range(2):
                        last = qtr == 3 and cp == 2 and i == 1
                        ev_q = remit2(lambda e, w_=w, kcw_=4 * qtr + cp + i,
                                      U_=LSET[i][0], V_=LSET[i][1]: e.tensor_tensor(
                            Wtile(w_, kcw_), U_[:], V_[:], ALU.add), drain=last)
                wf_free[0] = ev_q
            wslot_ready[w] = ev_q
            B.wait("sync", ev_q)
            ev_s = dma("sync", scratch_r[:, k0_base:k0_base + 16, c0:c0 + 512],
                       WSLOT(w).rearrange("p (k n) -> p k n", n=512), inc="sS" + str(w))
            scratch_ev[w] = ev_s

        w1T_r = w1T.rearrange("(k p) n -> p k n", p=128)
        w2T_r = w2T.rearrange("(k p) n -> p k n", p=128)
        wq1S_r = wq1S.rearrange("(k p) n -> p k n", p=128)
        wq2S_r = wq2S.rearrange("(k p) n -> p k n", p=128)

        def ffn1(p):
            ev_h = None
            for sl in range(16):
                if p == 0:
                    w = wslot_next[0]
                    wslot_next[0] = 1 - w
                    wq_halves(w1T_r, 512 * sl, 0, wq1S_r, 0, w)
                else:
                    w = load_wslice(wq1S_r[:, :, 512 * sl:512 * (sl + 1)], cast=False)
                for j in range(4):
                    hc = 4 * sl + j
                    bi = next_bank2()
                    pb = PSB[bi]
                    B.wait("tensor", psum_free[bi])
                    B.wait("tensor", wslot_ready[w])
                    ev_mm = None
                    for kc in range(16):
                        ev_mm = B.emit("tensor", lambda e, kc_=kc, j_=j, w_=w, pb_=pb: e.matmul(
                            pb_[:], Wtile(w_, kc_)[:, 128 * j_:128 * (j_ + 1)], X2Qc(kc_),
                            start=(kc_ == 0), stop=(kc_ == 15)),
                            inc="sT" if kc == 15 else None)
                    if j == 3:
                        wslot_free[w] = [ev_mm] + ([scratch_ev[w]] if scratch_ev[w] else [])
                        scratch_ev[w] = None
                    B.wait("scalar", ev_mm)
                    ev_g = B.emit("scalar", lambda e, hc_=hc, pb_=pb: e.activation(
                        Hc(hc_), pb_[:], AF.Gelu), inc="sA")
                    psum_free[bi] = ev_g
                    B.wait("vector", ev_g)
                    if hc == 0:
                        remit2(lambda e: e.tensor_copy(CMHB[:], Hc(0)))
                        ev_h = remit2(lambda e: e.tensor_scalar(
                            CMHB2[:], Hc(0), -1.0, None, ALU.mult))
                    else:
                        remit2(lambda e, hc_=hc: e.tensor_tensor(
                            CMHB[:], Hc(hc_), CMHB[:], ALU.max))
                        ev_h = remit2(lambda e, hc_=hc: e.scalar_tensor_tensor(
                            CMHB2[:], Hc(hc_), -1.0, CMHB2[:], ALU.mult, ALU.max))
            return ev_h

        def phase_qh(ev_cmh):
            B.wait("vector", ev_cmh)
            ev_cm = B.emit("vector", lambda e: e.tensor_tensor(
                CMHB[:], CMHB[:], CMHB2[:], ALU.max), inc="sV")
            B.wait("tensor", ev_cm)
            B.wait("tensor", trb_free[0])
            ev_t = None
            for wdx in range(4):
                ev_t = B.emit("tensor", lambda e, w_=wdx: e.transpose(
                    PSTRB[:, 128 * w_:128 * (w_ + 1)], CMHB[:, 128 * w_:128 * (w_ + 1)],
                    IDB[:]), inc="sT")
            B.wait("vector", ev_t)
            for wdx in range(4):
                B.emit("vector", lambda e, w_=wdx: e.tensor_reduce(
                    SMALL[:, w_:w_ + 1], PSTRB[:, 128 * w_:128 * (w_ + 1)], AX.X, ALU.max))
            ev = B.emit("vector", lambda e: e.drain(), inc="sV")
            trb_free[0] = ev
            ev_sm = scale_math()
            make_scale_bcasts(ev_sm)
            ev = None
            for hc in range(0, NHC, 2):
                ev = ladder_pair([(Hc(hc), Hc(hc)), (Hc(hc + 1), Hc(hc + 1))],
                                 flush=(hc == NHC - 2))
            return ev

        def ffn2(p):
            ev_stats = None
            for og in range(4):
                ev_mm = None
                for qt in range(4):
                    if p == 0:
                        w = wslot_next[0]
                        wslot_next[0] = 1 - w
                        wq_halves(w2T_r, 512 * og, 16 * qt, wq2S_r, 4, w)
                    else:
                        w = load_wslice(
                            wq2S_r[:, 16 * qt:16 * (qt + 1), 512 * og:512 * (og + 1)],
                            cast=False)
                    if qt == 0:
                        for bi in range(4):
                            B.wait("tensor", psum_free[bi])
                    B.wait("tensor", wslot_ready[w])
                    for kc in range(16):
                        for j in range(4):
                            last = (kc == 15 and j == 3)
                            ev_mm = B.emit("tensor", lambda e, kc_=kc, j_=j, qt_=qt, w_=w: e.matmul(
                                PSB[j_][:], Wtile(w_, kc_)[:, 128 * j_:128 * (j_ + 1)],
                                Hc(16 * qt_ + kc_),
                                start=(qt_ == 0 and kc_ == 0), stop=(qt_ == 3 and kc_ == 15)),
                                inc="sT" if last else None)
                    wslot_free[w] = [ev_mm] + ([scratch_ev[w]] if scratch_ev[w] else [])
                    scratch_ev[w] = None
                B.wait("vector", ev_mm)
                for j in range(4):
                    och = 4 * og + j
                    B.wait("vector", rbsq_free[0])
                    ev_r = B.emit("vector", lambda e, j_=j, och_=och: e.tensor_tensor(
                        Xc(och_), PSB[j_][:], Xc(och_), ALU.add), inc="sV", drain=False)
                    psum_free[j] = ev_r
                    B.wait("vector", ev_r)
                    B.emit("vector", lambda e, och_=och: e.tensor_copy(RB[:], Xc(och_)))
                    ev_sqb = B.emit("vector", lambda e, och_=och: e.tensor_tensor(
                        SQB[:], Xc(och_), Xc(och_), ALU.mult), inc="sV")
                    B.wait("tensor", ev_sqb)
                    B.wait("tensor", strips_free[0])
                    first = (og == 0 and j == 0)
                    lastt = (og == 3 and j == 3)
                    B.emit("tensor", lambda e, f_=first, l_=lastt: e.matmul(
                        PSSUM[:], ONE_COL_B, RB[:], start=f_, stop=l_))
                    ev_stats = B.emit("tensor", lambda e, f_=first, l_=lastt: e.matmul(
                        PSSQ[:], ONE_COL_B, SQB[:], start=f_, stop=l_), inc="sT")
                    rbsq_free[0] = ev_stats
            return ev_stats

        def phase_out(p, ev_ln2):
            B.wait("sync", ev_ln2)
            return dma("sync", outT.rearrange("(k p) n -> p k n", p=128)[:, :, TP * p:TP * (p + 1)],
                       X.rearrange("p (k n) -> p k n", n=512), inc="sO")

        out_ev = None
        for p in range(npass):
            hook["fn"] = (lambda: prepass_pump(8)) if p == 0 else None
            xb_ev = phase_x_load(p, out_ev)
            pump_hook()
            pump_hook()
            B.barrier()
            if p == 0:
                _dbg("xb", XBfull)
            phase_beta(xb_ev)
            pump_hook()
            B.barrier()
            if p == 0:
                _dbg("beta", BETA[:])
            phase_qk(wqT, qb, QNr)
            B.barrier()
            if p == 0:
                _dbg("qn", BIGA[:, 0:8192])
            phase_qk(wkT, kb, KNr)
            phase_v()
            B.barrier()
            if p == 0:
                _dbg("kn", BIGA[:, 8192:16384])
                _dbg("gv", BIGA[:, 16384:24576])
            phase_og()
            B.barrier()
            if p == 0:
                _dbg("gate", BIGA[:, 24576:32768])
            phase_att()
            B.barrier()
            if p == 0:
                _dbg("u", BIGA[:, 24576:32768])
            ev_stats = phase_oproj()
            B.barrier()
            if p == 0:
                _dbg("r1", X[:])
            phase_ln(ev_stats)
            pump_hook()
            B.barrier()
            if p == 0:
                _dbg("x2", X[:])
            phase_qx()
            if p == 0:
                prepass_pump(256)
            B.barrier()
            if p == 0:
                _dbg("x2q", BIGA[:, 32768:40960])
                _dbg("sbc", SBC[:])
                _dbg("ivbc", IVBC[:])
            ev_cmh = ffn1(p)
            B.barrier()
            if p == 0:
                _dbg("h", BIGA[:, 0:32768])
                _dbg("cmhb", CMHB[:])
            phase_qh(ev_cmh)
            B.barrier()
            if p == 0:
                _dbg("hq", BIGA[:, 0:32768])
            ev_stats2 = ffn2(p)
            B.barrier()
            if p == 0:
                _dbg("r2", X[:])
            if p == 0 and "wq2s" in DEBUG:
                t1 = nc.dram_tensor("dbg_wq1s", [D, DF], BF16, kind="ExternalOutput").ap()
                t2 = nc.dram_tensor("dbg_wq2s", [DF, D], BF16, kind="ExternalOutput").ap()
                B.barrier()
                ev1 = dma("sync", t1[:, :], wq1S[:, :], inc="sO")
                ev2 = dma("sync", t2[:, :], wq2S[:, :], inc="sO")
                for eng in ENGINES:
                    B.wait(eng, ev2)
            ev_ln2 = phase_ln(ev_stats2)
            out_ev = phase_out(p, ev_ln2)
            B.barrier()
        for name in list(B.cnt):
            if name.startswith(("sO", "sS")):
                B.wait("sync", B.now(name))

        with contextlib.ExitStack() as stack:
            sems = {name: stack.enter_context(nc.semaphore("sem_" + name)) for name in B.cnt}
            with nc.Block() as block:
                B.replay(block, sems)

    return nc


def _host_prep(inputs):
    f32 = np.float32
    x = np.asarray(inputs["L0_attn_hidden_states"], f32)[0]
    xT = np.ascontiguousarray(x.T)
    com = {
        "wqT": np.ascontiguousarray(np.asarray(inputs["L0_attn_q_w"], f32).T),
        "wkT": np.ascontiguousarray(np.asarray(inputs["L0_attn_k_w"], f32).T),
        "wvT": np.ascontiguousarray(np.asarray(inputs["L0_attn_v_w"], f32).T),
        "wogT": np.ascontiguousarray(np.asarray(inputs["L0_attn_og_w"], f32).T),
        "woT": np.ascontiguousarray(np.asarray(inputs["L0_attn_o_w"], f32).T),
        "w1T": np.ascontiguousarray(np.asarray(inputs["L0_ffn_w1"], f32).T),
        "w2T": np.ascontiguousarray(np.asarray(inputs["L0_ffn_w2"], f32).T),
        "bwT": np.ascontiguousarray(np.asarray(inputs["L0_attn_b_w"], f32).T),
        "qb": np.ascontiguousarray(np.asarray(inputs["L0_attn_q_b"], f32)[None, :]),
        "kb": np.ascontiguousarray(np.asarray(inputs["L0_attn_k_b"], f32)[None, :]),
        "vb": np.ascontiguousarray(np.asarray(inputs["L0_attn_v_b"], f32)[None, :]),
        "ob": np.ascontiguousarray(np.asarray(inputs["L0_attn_o_b"], f32)[None, :]),
        "bb": np.ascontiguousarray(np.asarray(inputs["L0_attn_b_b"], f32)[None, :]),
    }
    com["idb"] = np.eye(128, dtype=ml_dtypes.bfloat16)
    com["idf"] = np.eye(128, dtype=f32)
    pos = np.arange(64, dtype=f32)
    dist = np.clip(pos[None, :] - pos[:, None], 0.0, None)
    decay = np.exp(-0.1 * dist).astype(f32)
    causal = pos[None, :] <= pos[:, None]
    m64 = np.where(causal, decay, 0.0).astype(f32)
    m128 = np.zeros((128, 128), f32)
    m128[0:64, 0:64] = m64.T
    m128[64:128, 64:128] = m64.T
    com["maskf"] = m128
    com["onesb"] = np.ones((128, 512), dtype=ml_dtypes.bfloat16)
    com["onesf"] = np.ones((8, 128), dtype=f32)
    com["onecf"] = np.ones((128, 1), dtype=f32)
    return xT, com


def kernel(**inputs):
    xT, com = _host_prep(inputs)
    L = xT.shape[1]
    tpc = L // NCORES
    nc = build_program(tpc)
    in_maps = []
    for c in range(NCORES):
        m = dict(com)
        m["xT"] = np.ascontiguousarray(xT[:, c * tpc:(c + 1) * tpc])
        in_maps.append(m)
    res = run_bass_kernel_spmd(nc, in_maps, core_ids=list(range(NCORES)))
    parts = [np.asarray(r["outT"]).T for r in res.results]
    out = np.concatenate(parts, axis=0)[None, :, :]
    return np.ascontiguousarray(out, dtype=np.float32)


# version 10 (final)
# speedup vs baseline: 1.0064x; 1.0064x over previous
import contextlib
import numpy as np
import ml_dtypes
import concourse.bass as bass
import concourse.mybir as mybir
from concourse.bass_utils import run_bass_kernel_spmd

F32 = mybir.dt.float32
BF16 = mybir.dt.bfloat16
AF = mybir.ActivationFunctionType
ALU = mybir.AluOpType
AX = mybir.AxisListType

NCORES = 8
D = 2048
DF = 8192
NH = 16
TP = 512
NKC = D // 128
NHC = DF // 128
EPS_L2 = 1.0e-12
SCALE = float(128) ** 0.5
E2_L2 = (SCALE * EPS_L2) ** 2
LN_EPS = 1.0e-5
Q_EPS = 1.0e-5
LADDER = [128.0, 64.0, 32.0, 16.0, 8.0, 4.0, 2.0, 1.0]
ENGINES = ("sync", "vector", "scalar", "tensor", "gpsimd")
DEBUG = ()
SAFE_QK = False
ACT_FUDGE = 0


class Builder:
    def __init__(self):
        self.prog = {e: [] for e in ENGINES}
        self.cnt = {}
        self.last_ev = {}

    def _ensure(self, name):
        if name not in self.cnt:
            self.cnt[name] = 0

    def wait(self, eng, ev):
        if ev is None:
            return
        name, val = ev
        if val <= 0:
            return
        self._ensure(name)
        self.prog[eng].append(("w", name, val))

    def emit(self, eng, fn, inc=None, k=1, drain=True):
        if inc is None:
            self.prog[eng].append(("d", fn))
            return None
        self._ensure(inc)
        self.cnt[inc] += k
        if eng in ("vector", "scalar") and drain:
            self.prog[eng].append(("d", fn))
            self.prog[eng].append(("i", lambda e: e.drain(), inc, k))
        else:
            self.prog[eng].append(("i", fn, inc, k))
        if not inc.endswith("p"):
            self.last_ev[eng] = (inc, self.cnt[inc])
        return (inc, self.cnt[inc])

    def now(self, name):
        self._ensure(name)
        return (name, self.cnt[name])

    def barrier(self, engines=ENGINES):
        evs = list(self.last_ev.values())
        for eng in engines:
            for ev in evs:
                self.wait(eng, ev)

    def replay(self, block, sems):
        def make(eng):
            def body(e):
                for it in self.prog[eng]:
                    if it[0] == "w":
                        e.wait_ge(sems[it[1]], it[2])
                    elif it[0] == "d":
                        it[1](e)
                    else:
                        it[1](e).then_inc(sems[it[2]], it[3])
            return body
        for eng in ENGINES:
            getattr(block, eng)(make(eng))


def build_program(tpc):
    npass = tpc // TP
    nc = bass.Bass("TRN2", target_bir_lowering=False)

    xT = nc.dram_tensor("xT", [D, tpc], F32, kind="ExternalInput").ap()
    wqT = nc.dram_tensor("wqT", [D, D], F32, kind="ExternalInput").ap()
    wkT = nc.dram_tensor("wkT", [D, D], F32, kind="ExternalInput").ap()
    wvT = nc.dram_tensor("wvT", [D, D], F32, kind="ExternalInput").ap()
    wogT = nc.dram_tensor("wogT", [D, D], F32, kind="ExternalInput").ap()
    woT = nc.dram_tensor("woT", [D, D], F32, kind="ExternalInput").ap()
    w1T = nc.dram_tensor("w1T", [D, DF], F32, kind="ExternalInput").ap()
    w2T = nc.dram_tensor("w2T", [DF, D], F32, kind="ExternalInput").ap()
    bwT = nc.dram_tensor("bwT", [D, NH], F32, kind="ExternalInput").ap()
    qb = nc.dram_tensor("qb", [1, D], F32, kind="ExternalInput").ap()
    kb = nc.dram_tensor("kb", [1, D], F32, kind="ExternalInput").ap()
    vb = nc.dram_tensor("vb", [1, D], F32, kind="ExternalInput").ap()
    ob = nc.dram_tensor("ob", [1, D], F32, kind="ExternalInput").ap()
    bb = nc.dram_tensor("bb", [1, NH], F32, kind="ExternalInput").ap()
    idb = nc.dram_tensor("idb", [128, 128], BF16, kind="ExternalInput").ap()
    idf = nc.dram_tensor("idf", [128, 128], F32, kind="ExternalInput").ap()
    maskf = nc.dram_tensor("maskf", [128, 128], F32, kind="ExternalInput").ap()
    onesb = nc.dram_tensor("onesb", [128, 512], BF16, kind="ExternalInput").ap()
    onesf = nc.dram_tensor("onesf", [8, 128], F32, kind="ExternalInput").ap()
    onecf = nc.dram_tensor("onecf", [128, 1], F32, kind="ExternalInput").ap()
    outT = nc.dram_tensor("outT", [D, tpc], F32, kind="ExternalOutput").ap()
    wq1S = nc.dram_tensor("wq1S", [D, DF], BF16, kind="Internal").ap()
    wq2S = nc.dram_tensor("wq2S", [DF, D], BF16, kind="Internal").ap()

    B = Builder()

    _st = contextlib.ExitStack()
    with _st:
        X = _st.enter_context(nc.sbuf_tensor("Xs", [128, NKC * TP], F32))
        BIGA = _st.enter_context(nc.sbuf_tensor("BIGAs", [128, 40960], BF16))
        WB = _st.enter_context(nc.sbuf_tensor("WBs", [128, 16384], BF16))
        WF = _st.enter_context(nc.sbuf_tensor("WFs", [128, 2048], F32))
        BETA = _st.enter_context(nc.sbuf_tensor("BETAs", [128, 64], F32))
        UT = _st.enter_context(nc.sbuf_tensor("UTs", [128, 512], F32))
        VT = _st.enter_context(nc.sbuf_tensor("VTs", [128, 512], F32))
        GT = _st.enter_context(nc.sbuf_tensor("GTs", [128, 512], F32))
        SBC = _st.enter_context(nc.sbuf_tensor("SBCs", [128, 512], F32))
        IVBC = _st.enter_context(nc.sbuf_tensor("IVBCs", [128, 512], F32))
        QTMB = _st.enter_context(nc.sbuf_tensor("QTMBs", [128, 512], BF16))
        RB = _st.enter_context(nc.sbuf_tensor("RBs", [128, 512], BF16))
        SQB = _st.enter_context(nc.sbuf_tensor("SQBs", [128, 512], BF16))
        CMHB = _st.enter_context(nc.sbuf_tensor("CMHBs", [128, 512], BF16))
        MSA = _st.enter_context(nc.sbuf_tensor("MSAs", [128, 128], BF16))
        MSB = _st.enter_context(nc.sbuf_tensor("MSBs", [128, 128], BF16))
        SMALL = _st.enter_context(nc.sbuf_tensor("SMALLs", [128, 64], F32))
        RMSC = _st.enter_context(nc.sbuf_tensor("RMSCs", [128, 8], F32))
        T8 = _st.enter_context(nc.sbuf_tensor("T8s", [8, 128], F32))
        UT2 = _st.enter_context(nc.sbuf_tensor("UT2s", [128, 512], F32))
        VT2 = _st.enter_context(nc.sbuf_tensor("VT2s", [128, 512], F32))
        GT2 = _st.enter_context(nc.sbuf_tensor("GT2s", [128, 512], F32))
        CMHB2 = _st.enter_context(nc.sbuf_tensor("CMHB2s", [128, 512], BF16))
        SQF2 = _st.enter_context(nc.sbuf_tensor("SQF2s", [128, 1024], F32))
        NEGH = _st.enter_context(nc.sbuf_tensor("NEGHs", [128, 4], F32))
        HALFP = _st.enter_context(nc.sbuf_tensor("HALFPs", [128, 1], F32))
        STR = _st.enter_context(nc.sbuf_tensor("STRs", [1, 3072], F32))
        SM2 = _st.enter_context(nc.sbuf_tensor("SM2s", [1, 64], F32))
        IDB = _st.enter_context(nc.sbuf_tensor("IDBs", [128, 128], BF16))
        IDF = _st.enter_context(nc.sbuf_tensor("IDFs", [128, 128], F32))
        MASK = _st.enter_context(nc.sbuf_tensor("MASKs", [128, 128], F32))
        ONESB = _st.enter_context(nc.sbuf_tensor("ONESBs", [128, 512], BF16))
        ONESF = _st.enter_context(nc.sbuf_tensor("ONESFs", [8, 128], F32))
        ONECF = _st.enter_context(nc.sbuf_tensor("ONECFs", [128, 1], F32))
        SM3 = _st.enter_context(nc.sbuf_tensor("SM3s", [1, 16], F32))
        HALFC = _st.enter_context(nc.sbuf_tensor("HALFCs", [1, 1], F32))
        BWTB = _st.enter_context(nc.sbuf_tensor("BWTBs", [128, 256], BF16))
        BROW = _st.enter_context(nc.sbuf_tensor("BROWs", [1, D], BF16))
        BBB = _st.enter_context(nc.sbuf_tensor("BBBs", [1, NH], BF16))
        PS0 = _st.enter_context(nc.psum_tensor("PS0", [128, 512], F32))
        PS1 = _st.enter_context(nc.psum_tensor("PS1", [128, 512], F32))
        PS2 = _st.enter_context(nc.psum_tensor("PS2", [128, 512], F32))
        PS3 = _st.enter_context(nc.psum_tensor("PS3", [128, 512], F32))
        PSSUM = _st.enter_context(nc.psum_tensor("PSSUM", [1, 512], F32))
        PSSQ = _st.enter_context(nc.psum_tensor("PSSQ", [1, 512], F32))
        PSTRB = _st.enter_context(nc.psum_tensor("PSTRB", [128, 512], BF16))
        PSTRF = _st.enter_context(nc.psum_tensor("PSTRF", [128, 512], F32))
        PSB = [PS0, PS1, PS2, PS3]
        CM = GT
        SQS = UT

        QNr = BIGA[:, 0:8192].rearrange("p (h n) -> p h n", n=512)
        KNr = BIGA[:, 8192:16384].rearrange("p (h n) -> p h n", n=512)

        def QN(h):
            return BIGA[:, 512 * h:512 * (h + 1)]

        def KN(h):
            return BIGA[:, 8192 + 512 * h:8192 + 512 * (h + 1)]

        def GVT(tt):
            return BIGA[:, 16384 + 2048 * tt:16384 + 2048 * (tt + 1)]

        def USL(h):
            return BIGA[:, 24576 + 512 * h:24576 + 512 * (h + 1)]

        def XBc(kc):
            return BIGA[:, 32768 + 512 * kc:32768 + 512 * (kc + 1)]

        XBfull = BIGA[:, 32768:40960]
        X2Qc = XBc

        def Hc(hc):
            return BIGA[:, 512 * hc:512 * (hc + 1)]

        def Xc(kc):
            return X[:, 512 * kc:512 * (kc + 1)]

        def WSLOT(w):
            return WB[:, 8192 * w:8192 * (w + 1)]

        def Wtile(w, kc):
            return WB[:, 8192 * w + 512 * kc:8192 * w + 512 * (kc + 1)]

        def STRr(r):
            return STR[0:1, 512 * r:512 * (r + 1)]

        psum_free = {i: None for i in range(4)}
        strips_free = [None]
        trb_free = [None]
        trf_free = [None]
        wslot_free = {0: [], 1: []}
        scratch_ev = {0: None, 1: None}
        wslot_ready = {0: None, 1: None}
        wslot_next = [0]
        wf_free = [None]
        wf2_free = [None]
        sqs_free = {0: None, 1: None}
        qtmb_free = {0: None, 1: None}
        rbsq_free = [None]
        ms_free = {0: None, 1: None}
        bank_i = [0]
        rv_hist = [None, None]
        brow_free = [None]

        def next_bank2():
            b = bank_i[0] % 4
            bank_i[0] += 1
            return b

        def remit2(fn, drain=False):
            B.wait("vector", rv_hist[0])
            ev = B.emit("vector", fn, inc="sV", drain=drain)
            rv_hist[0], rv_hist[1] = rv_hist[1], ev
            return ev

        def waitall(eng, evs):
            for ev in evs:
                B.wait(eng, ev)

        gdma_last = {}

        def dma(eng, out_ap, in_ap, inc="sD"):
            if eng == "gpsimd":
                inc = inc + "g"
                B.wait("gpsimd", gdma_last.get(inc))
            ev = B.emit(eng, lambda e, o=out_ap, i=in_ap: e.dma_start(out=o, in_=i), inc=inc, k=16)
            if eng == "gpsimd":
                gdma_last[inc] = ev
            return ev

        def cs(wt, c0, cw, k0=None, kn=None):
            r = wt.rearrange("(k p) n -> p k n", p=128)
            if k0 is None:
                return r[:, :, c0:c0 + cw]
            return r[:, k0:k0 + kn, c0:c0 + cw]

        def load_wslice(in_ap, cast, colw=512):
            w = wslot_next[0]
            wslot_next[0] = 1 - w
            eng = "gpsimd" if cast else "sync"
            waitall(eng, wslot_free[w])
            ev = dma(eng, WSLOT(w).rearrange("p (k n) -> p k n", n=colw), in_ap, inc="sW" + str(w))
            wslot_ready[w] = ev
            return w

        def _dbg(name, ap):
            if name not in DEBUG:
                return
            sh = list(ap.shape)
            t = nc.dram_tensor("dbg_" + name, sh, ap.dtype, kind="ExternalOutput").ap()
            B.barrier()
            idx = tuple(slice(0, s) for s in sh)
            ev = dma("sync", t[idx], ap, inc="sO")
            for eng in ENGINES:
                B.wait(eng, ev)

        dma("sync", IDB[:], idb[:, :], inc="sC")
        dma("sync", IDF[:], idf[:, :], inc="sC")
        dma("sync", MASK[:], maskf[:, :], inc="sC")
        dma("sync", ONESB[:], onesb[:, :], inc="sC")
        dma("sync", ONESF[:], onesf[:, :], inc="sC")
        dma("sync", ONECF[:], onecf[:, :], inc="sC")
        dma("gpsimd", BBB[:], bb[:, :], inc="sC")
        dma("gpsimd", BWTB[:].rearrange("p (k n) -> p k n", n=NH),
            bwT.rearrange("(k p) n -> p k n", p=128), inc="sC")
        B.emit("gpsimd", lambda e: e.memset(NEGH[:], -0.5), inc="sC", k=1)
        B.emit("gpsimd", lambda e: e.memset(HALFC[:], 0.5), inc="sC", k=1)
        B.emit("gpsimd", lambda e: e.memset(HALFP[:], 0.5), inc="sC", k=1)
        for ev in (B.now("sC"), B.now("sCg")):
            for eng in ENGINES:
                B.wait(eng, ev)

        ONE_ROW_B = ONESB[0:1, :]
        ONE_COL_B = ONESB[:, 0:1]

        pp_quarters = []
        for m, nkq in ((0, 8), (1, 32)):
            for kq in range(nkq):
                for cq in range(128 // nkq):
                    pp_quarters.append((m, kq, cq))
        w_r = {0: w1T.rearrange("(k p) n -> p k n", p=128),
               1: w2T.rearrange("(k p) n -> p k n", p=128)}
        pp = {"pos": 0, "qidx": 0, "ev_mm": None}
        rmsc_ev = [None, None]

        def pp_finalize(m):
            B.emit("vector", lambda e: e.tensor_reduce(
                SM3[0:1, 8:10], SM2[0:1, 0:16].rearrange("p (s n) -> p s n", n=8), AX.X, ALU.add))
            B.emit("vector", lambda e: e.drain())
            B.emit("vector", lambda e: e.tensor_reduce(SM3[0:1, 0:1], SM3[0:1, 8:10], AX.X, ALU.add))
            B.emit("vector", lambda e: e.drain())
            ev = B.emit("vector", lambda e: e.tensor_scalar(
                SM3[0:1, 1:2], SM3[0:1, 0:1], 1.0 / float(D * DF), Q_EPS, ALU.mult, ALU.max),
                inc="sVp")
            B.wait("gpsimd", ev)
            ev = B.emit("gpsimd", lambda e: e.tensor_tensor(
                SM3[0:1, 2:3], SM3[0:1, 1:2], HALFC[:], ALU.pow), inc="sGp")
            B.wait("vector", ev)
            B.emit("vector", lambda e: e.tensor_scalar(SM3[0:1, 4:5], SM3[0:1, 2:3], 0.5, None, ALU.mult))
            B.emit("vector", lambda e: e.tensor_scalar(SM3[0:1, 5:6], SM3[0:1, 2:3], -0.5, None, ALU.mult))
            B.emit("vector", lambda e: e.tensor_scalar(SM3[0:1, 6:7], SM3[0:1, 2:3], 1.0, None, ALU.mult))
            ev = B.emit("vector", lambda e: e.tensor_scalar(
                SM3[0:1, 7:8], SM3[0:1, 2:3], -1.0, None, ALU.mult), inc="sVp")
            B.wait("tensor", ev)
            B.wait("tensor", trf_free[0])
            ev = B.emit("tensor", lambda e: e.matmul(
                PSTRF[:, 0:4], ONESF[0:1, :], SM3[0:1, 4:8], start=True, stop=True), inc="sTp")
            B.wait("vector", ev)
            ev = B.emit("vector", lambda e, m_=m: e.tensor_copy(
                RMSC[:, 4 * m_:4 * m_ + 4], PSTRF[:, 0:4]), inc="sVp")
            trf_free[0] = ev
            rmsc_ev[m] = ev

        def prepass_pump(n):
            assert n % 8 == 0
            for _ in range(n):
                if pp["pos"] >= len(pp_quarters):
                    return
                m, kq, cq = pp_quarters[pp["pos"]]
                pp["pos"] += 1
                odd = pp["qidx"] % 2
                buf = WF[:, 1024:2048] if odd else WF[:, 0:1024]
                guard = wf2_free if odd else wf_free
                B.wait("sync", guard[0])
                ev_wf = dma("sync", buf.rearrange("p (k n) -> p k n", n=512),
                            w_r[m][:, 2 * kq:2 * kq + 2, 512 * cq:512 * (cq + 1)],
                            inc="sFBp" if odd else "sFp")
                B.wait("scalar", ev_wf)
                B.wait("scalar", pp["ev_mm"])
                ev_sq = B.emit("scalar", lambda e, b_=buf: e.activation(SQF2[:], b_, AF.Square),
                               inc="sAp")
                guard[0] = ev_sq
                B.wait("tensor", ev_sq)
                grp = pp["qidx"] % 8
                if grp == 0:
                    B.wait("tensor", strips_free[0])
                ev = None
                for c4 in range(2):
                    ev = B.emit("tensor", lambda e, c_=c4, g_=grp: e.matmul(
                        PSSUM[:], ONECF[:], SQF2[:, 512 * c_:512 * (c_ + 1)],
                        start=(g_ == 0 and c_ == 0), stop=(g_ == 7 and c_ == 1)), inc="sTp")
                pp["ev_mm"] = ev
                if grp == 7:
                    B.wait("vector", ev)
                    B.emit("vector", lambda e: e.tensor_reduce(
                        SM3[0:1, 8:12], PSSUM[0:1, :].rearrange("p (s n) -> p s n", n=128),
                        AX.X, ALU.add))
                    B.emit("vector", lambda e: e.drain())
                    ev_str = B.emit("vector", lambda e, s_=(pp["qidx"] // 8) % 16: e.tensor_reduce(
                        SM2[0:1, s_:s_ + 1], SM3[0:1, 8:12], AX.X, ALU.add), inc="sVp")
                    strips_free[0] = ev_str
                pp["qidx"] += 1
                if pp["qidx"] in (128, 256):
                    pp_finalize(0 if pp["qidx"] == 128 else 1)
        hook = {"fn": None}

        def pump_hook():
            if hook["fn"]:
                hook["fn"]()

        def phase_x_load(p, prev_out_ev):
            B.wait("sync", prev_out_ev)
            ev = dma("sync", X.rearrange("p (k n) -> p k n", n=512),
                     xT.rearrange("(k p) n -> p k n", p=128)[:, :, TP * p:TP * (p + 1)], inc="sX")
            B.wait("vector", ev)
            return B.emit("vector", lambda e: e.tensor_copy(XBfull, X[:]), inc="sV")

        def phase_beta(xb_ev):
            B.wait("tensor", xb_ev)
            ev = None
            for tt in range(4):
                for kc in range(16):
                    B.emit("tensor", lambda e, tt_=tt, kc_=kc: e.matmul(
                        PS2[:, 16 * tt_:16 * (tt_ + 1)],
                        XBc(kc_)[:, 128 * tt_:128 * (tt_ + 1)],
                        BWTB[:, NH * kc_:NH * (kc_ + 1)],
                        start=(kc_ == 0), stop=False))
                ev = B.emit("tensor", lambda e, tt_=tt: e.matmul(
                    PS2[:, 16 * tt_:16 * (tt_ + 1)], ONE_ROW_B[:, 0:128], BBB[0:1, :],
                    start=False, stop=True), inc="sT")
            B.wait("scalar", ev)
            B.emit("scalar", lambda e: e.activation(SMALL[:, 0:64], PS2[:, 0:64], AF.Abs), inc="sA")
            B.emit("scalar", lambda e: e.activation(SMALL[:, 0:64], SMALL[:, 0:64], AF.Exp, scale=-1.0))
            ev = B.emit("scalar", lambda e: e.activation(BETA[:], PS2[:, 0:64], AF.Relu), inc="sA")
            psum_free[2] = ev
            B.wait("vector", ev)
            ev = B.emit("vector", lambda e: e.tensor_scalar(
                SMALL[:, 0:64], SMALL[:, 0:64], 1.0, None, ALU.add), inc="sV")
            B.wait("scalar", ev)
            ev = B.emit("scalar", lambda e: e.activation(SMALL[:, 0:64], SMALL[:, 0:64], AF.Ln), inc="sA")
            B.wait("vector", ev)
            return B.emit("vector", lambda e: e.tensor_tensor(
                BETA[:], BETA[:], SMALL[:, 0:64], ALU.add), inc="sV")

        def phase_qk(wt, bias_dram, dest_r):
            B.wait("gpsimd", brow_free[0])
            ev_b = dma("gpsimd", BROW[:], bias_dram[:, :], inc="sC")
            B.wait("tensor", ev_b)
            ev_last = None
            for s in range(4):
                pump_hook()
                w = load_wslice(cs(wt, 512 * s, 512), cast=True)
                for tt in range(4):
                    bi = next_bank2()
                    pb = PSB[bi]
                    g = 16 * bi
                    SQ = UT if bi % 2 == 0 else VT
                    QT = QTMB if bi % 2 == 0 else RB
                    B.wait("tensor", psum_free[bi])
                    B.wait("tensor", wslot_ready[w])
                    for kc in range(16):
                        B.emit("tensor", lambda e, kc_=kc, tt_=tt, w_=w, pb_=pb: e.matmul(
                            pb_[:], XBc(kc_)[:, 128 * tt_:128 * (tt_ + 1)], Wtile(w_, kc_),
                            start=(kc_ == 0), stop=False))
                    ev_mm = B.emit("tensor", lambda e, pb_=pb, s_=s: e.matmul(
                        pb_[:], ONE_ROW_B[:, 0:128], BROW[0:1, 512 * s_:512 * (s_ + 1)],
                        start=False, stop=True), inc="sT")
                    ev_last = ev_mm
                    if tt == 3:
                        wslot_free[w] = [ev_mm]
                    B.wait("scalar", ev_mm)
                    B.wait("scalar", sqs_free[bi % 2])
                    ev_sq = B.emit("scalar", lambda e, pb_=pb, SQ_=SQ: e.activation(
                        SQ_[:], pb_[:], AF.Square), inc="sA")
                    B.wait("vector", ev_sq)
                    B.emit("vector", lambda e, SQ_=SQ, g_=g: e.tensor_reduce(
                        SMALL[:, g_:g_ + 4], SQ_[:].rearrange("p (s n) -> p s n", n=128),
                        AX.X, ALU.add))
                    ev_red = B.emit("vector", lambda e: e.drain(), inc="sV")
                    sqs_free[bi % 2] = ev_red
                    ev_max = B.emit("vector", lambda e, g_=g: e.tensor_scalar(
                        SMALL[:, g_ + 4:g_ + 8], SMALL[:, g_:g_ + 4], E2_L2, None, ALU.max), inc="sV")
                    B.wait("gpsimd", ev_max)
                    ev_r = B.emit("gpsimd", lambda e, g_=g: e.tensor_tensor(
                        SMALL[:, g_ + 8:g_ + 12], SMALL[:, g_ + 4:g_ + 8], NEGH[:], ALU.pow), inc="sG")
                    B.wait("vector", ev_r)
                    B.wait("vector", qtmb_free[bi % 2])
                    ev_q = None
                    for j in range(4):
                        ev_q = B.emit("vector", lambda e, j_=j, pb_=pb, g_=g, QT_=QT: e.tensor_scalar(
                            QT_[:, 128 * j_:128 * (j_ + 1)], pb_[:, 128 * j_:128 * (j_ + 1)],
                            SMALL[:, g_ + 8 + j_:g_ + 9 + j_], None, ALU.mult),
                            inc="sV" if j == 3 else None)
                    psum_free[bi] = ev_q
                    B.wait("tensor", ev_q)
                    B.wait("tensor", trb_free[0])
                    ev_t = None
                    for j in range(4):
                        ev_t = B.emit("tensor", lambda e, j_=j, QT_=QT: e.transpose(
                            PSTRB[:, 128 * j_:128 * (j_ + 1)], QT_[:, 128 * j_:128 * (j_ + 1)],
                            IDB[:]), inc="sT")
                    qtmb_free[bi % 2] = ev_t
                    B.wait("vector", ev_t)
                    ev_e = B.emit("vector", lambda e, s_=s, tt_=tt, dr_=dest_r: e.tensor_copy(
                        dr_[:, 4 * s_:4 * s_ + 4, 128 * tt_:128 * (tt_ + 1)],
                        PSTRB[:].rearrange("p (j n) -> p j n", n=128)), inc="sV",
                        drain=(s == 3 and tt == 3))
                    trb_free[0] = ev_e
            brow_free[0] = ev_last

        def phase_v():
            B.wait("gpsimd", brow_free[0])
            ev_b = dma("gpsimd", BROW[:], vb[:, :], inc="sC")
            B.wait("tensor", ev_b)
            ev_last = None
            for s in range(4):
                pump_hook()
                w = load_wslice(cs(wvT, 512 * s, 512), cast=True)
                for tt in range(4):
                    bi = next_bank2()
                    pb = PSB[bi]
                    B.wait("tensor", psum_free[bi])
                    B.wait("tensor", wslot_ready[w])
                    for kc in range(16):
                        B.emit("tensor", lambda e, kc_=kc, tt_=tt, w_=w, pb_=pb: e.matmul(
                            pb_[:], XBc(kc_)[:, 128 * tt_:128 * (tt_ + 1)], Wtile(w_, kc_),
                            start=(kc_ == 0), stop=False))
                    ev_mm = B.emit("tensor", lambda e, pb_=pb, s_=s: e.matmul(
                        pb_[:], ONE_ROW_B[:, 0:128], BROW[0:1, 512 * s_:512 * (s_ + 1)],
                        start=False, stop=True), inc="sT")
                    ev_last = ev_mm
                    if tt == 3:
                        wslot_free[w] = [ev_mm]
                    B.wait("vector", ev_mm)
                    ev_gv = None
                    for j in range(4):
                        h = 4 * s + j
                        ev_gv = B.emit("vector", lambda e, j_=j, tt_=tt, h_=h, pb_=pb: e.tensor_scalar(
                            GVT(tt_)[:, 128 * h_:128 * (h_ + 1)],
                            pb_[:, 128 * j_:128 * (j_ + 1)],
                            BETA[:, 16 * tt_ + h_:16 * tt_ + h_ + 1], None, ALU.mult),
                            inc="sV" if j == 3 else None,
                            drain=(s == 3 and tt == 3))
                    psum_free[bi] = ev_gv
            brow_free[0] = ev_last

        def phase_og():
            for s in range(4):
                pump_hook()
                w = load_wslice(cs(wogT, 512 * s, 512), cast=True)
                for j in range(4):
                    h = 4 * s + j
                    bi = next_bank2()
                    pb = PSB[bi]
                    B.wait("tensor", psum_free[bi])
                    B.wait("tensor", wslot_ready[w])
                    ev_mm = None
                    for kc in range(16):
                        ev_mm = B.emit("tensor", lambda e, kc_=kc, j_=j, w_=w, pb_=pb: e.matmul(
                            pb_[:], Wtile(w_, kc_)[:, 128 * j_:128 * (j_ + 1)], XBc(kc_),
                            start=(kc_ == 0), stop=(kc_ == 15)),
                            inc="sT" if kc == 15 else None)
                    if j == 3:
                        wslot_free[w] = [ev_mm]
                    B.wait("scalar", ev_mm)
                    ev_si = B.emit("scalar", lambda e, h_=h, pb_=pb: e.activation(
                        USL(h_), pb_[:], AF.Silu), inc="sA")
                    psum_free[bi] = ev_si

        def phase_att():
            i = 0
            for tt in range(4):
                pump_hook()
                for h in range(NH):
                    a = i % 2
                    pb_s = PSB[a]
                    pb_av = PSB[2 + a]
                    MS = MSA if a == 0 else MSB
                    B.wait("tensor", psum_free[a])
                    ev_s = B.emit("tensor", lambda e, tt_=tt, h_=h, pb_=pb_s: e.matmul(
                        pb_[:, 0:128], KN(h_)[:, 128 * tt_:128 * (tt_ + 1)],
                        QN(h_)[:, 128 * tt_:128 * (tt_ + 1)], start=True, stop=True), inc="sT")
                    B.wait("vector", ev_s)
                    B.wait("vector", ms_free[a])
                    ev_ms = B.emit("vector", lambda e, pb_=pb_s, MS_=MS: e.tensor_tensor(
                        MS_[:], pb_[:, 0:128], MASK[:], ALU.mult), inc="sV")
                    psum_free[a] = ev_ms
                    B.wait("tensor", ev_ms)
                    B.wait("tensor", psum_free[2 + a])
                    ev_a = B.emit("tensor", lambda e, tt_=tt, h_=h, pb_=pb_av, MS_=MS: e.matmul(
                        pb_[:, 0:128], GVT(tt_)[:, 128 * h_:128 * (h_ + 1)], MS_[:],
                        start=True, stop=True), inc="sT")
                    ms_free[a] = ev_a
                    B.wait("vector", ev_a)
                    ev_u = B.emit("vector", lambda e, tt_=tt, h_=h, pb_=pb_av: e.tensor_tensor(
                        USL(h_)[:, 128 * tt_:128 * (tt_ + 1)], pb_[:, 0:128],
                        USL(h_)[:, 128 * tt_:128 * (tt_ + 1)], ALU.mult), inc="sV",
                        drain=(i == 63))
                    psum_free[2 + a] = ev_u
                    i += 1

        def phase_oproj():
            B.wait("gpsimd", brow_free[0])
            ev_b = dma("gpsimd", BROW[:], ob[:, :], inc="sC")
            B.wait("tensor", ev_b)
            ev_stats = None
            for s in range(4):
                w = load_wslice(cs(woT, 512 * s, 512), cast=True)
                for j in range(4):
                    och = 4 * s + j
                    bi = next_bank2()
                    pb = PSB[bi]
                    B.wait("tensor", psum_free[bi])
                    B.wait("tensor", wslot_ready[w])
                    for kc in range(16):
                        B.emit("tensor", lambda e, kc_=kc, j_=j, w_=w, pb_=pb: e.matmul(
                            pb_[:], Wtile(w_, kc_)[:, 128 * j_:128 * (j_ + 1)], USL(kc_),
                            start=(kc_ == 0), stop=False))
                    ev_mm = B.emit("tensor", lambda e, pb_=pb, och_=och: e.matmul(
                        pb_[:], BROW[0:1, 128 * och_:128 * (och_ + 1)], ONE_ROW_B,
                        start=False, stop=True), inc="sT")
                    if j == 3:
                        wslot_free[w] = [ev_mm]
                    B.wait("vector", ev_mm)
                    B.wait("vector", rbsq_free[0])
                    ev_r = B.emit("vector", lambda e, pb_=pb, och_=och: e.tensor_tensor(
                        Xc(och_), pb_[:], Xc(och_), ALU.add), inc="sV", drain=False)
                    psum_free[bi] = ev_r
                    B.wait("vector", ev_r)
                    B.emit("vector", lambda e, och_=och: e.tensor_copy(RB[:], Xc(och_)))
                    ev_sqb = B.emit("vector", lambda e, och_=och: e.tensor_tensor(
                        SQB[:], Xc(och_), Xc(och_), ALU.mult), inc="sV")
                    B.wait("tensor", ev_sqb)
                    B.wait("tensor", strips_free[0])
                    B.emit("tensor", lambda e, och_=och: e.matmul(
                        PSSUM[:], ONE_COL_B, RB[:], start=(och_ == 0), stop=(och_ == 15)))
                    ev_stats = B.emit("tensor", lambda e, och_=och: e.matmul(
                        PSSQ[:], ONE_COL_B, SQB[:], start=(och_ == 0), stop=(och_ == 15)), inc="sT")
                    rbsq_free[0] = ev_stats
            brow_free[0] = ev_mm
            return ev_stats

        def phase_ln(ev_stats):
            B.wait("vector", ev_stats)
            B.emit("vector", lambda e: e.tensor_scalar(
                STRr(0), PSSUM[0:1, :], 1.0 / D, None, ALU.mult))
            ev_strips = B.emit("vector", lambda e: e.tensor_scalar(
                STRr(1), PSSQ[0:1, :], 1.0 / D, None, ALU.mult), inc="sV")
            strips_free[0] = ev_strips
            B.emit("vector", lambda e: e.tensor_tensor(STRr(2), STRr(0), STRr(0), ALU.mult), inc="sV")
            B.emit("vector", lambda e: e.tensor_tensor(STRr(2), STRr(1), STRr(2), ALU.subtract), inc="sV")
            B.emit("vector", lambda e: e.tensor_scalar(STRr(3), STRr(2), LN_EPS, None, ALU.add), inc="sV")
            B.emit("vector", lambda e: e.reciprocal(STRr(4), STRr(3)))
            ev = B.emit("vector", lambda e: e.drain(), inc="sV")
            B.wait("scalar", ev)
            ev = B.emit("scalar", lambda e: e.activation(STRr(5), STRr(4), AF.Sqrt), inc="sA")
            B.wait("tensor", ev)
            B.wait("tensor", ev_strips)
            B.wait("tensor", trf_free[0])
            ev = B.emit("tensor", lambda e: e.matmul(
                PSTRF[:], ONESF[0:1, :], STRr(0), start=True, stop=True), inc="sT")
            B.wait("vector", ev)
            ev = B.emit("vector", lambda e: e.tensor_copy(SBC[:], PSTRF[:]), inc="sV")
            B.wait("tensor", ev)
            ev = B.emit("tensor", lambda e: e.matmul(
                PSTRF[:], ONESF[0:1, :], STRr(5), start=True, stop=True), inc="sT")
            B.wait("vector", ev)
            ev = B.emit("vector", lambda e: e.tensor_copy(IVBC[:], PSTRF[:]), inc="sV")
            trf_free[0] = ev
            ev_ap = None
            for kc0 in range(0, 16, 2):
                for i in range(2):
                    remit2(lambda e, kc_=kc0 + i, G_=(GT, GT2)[i]: e.tensor_tensor(
                        G_[:], Xc(kc_), SBC[:], ALU.subtract))
                for i in range(2):
                    last = kc0 + i == 15
                    ev_ap = remit2(lambda e, kc_=kc0 + i, G_=(GT, GT2)[i]: e.tensor_tensor(
                        Xc(kc_), G_[:], IVBC[:], ALU.mult), drain=last)
            return ev_ap

        def make_scale_bcasts(ev_small):
            B.wait("tensor", ev_small)
            B.wait("tensor", trf_free[0])
            for half, dstc in ((0, 4), (1, 5)):
                ev = None
                for wdx in range(4):
                    ev = B.emit("tensor", lambda e, w_=wdx, h_=half: e.transpose(
                        PSTRF[0:1, 128 * w_:128 * (w_ + 1)],
                        SMALL[:, 16 + 4 * h_ + w_:17 + 4 * h_ + w_], IDF[:]), inc="sT")
                B.wait("vector", ev)
                ev = B.emit("vector", lambda e, d_=dstc: e.tensor_copy(
                    STRr(d_), PSTRF[0:1, 0:512]), inc="sV")
                B.wait("tensor", ev)
            ev = B.emit("tensor", lambda e: e.matmul(
                PSTRF[:], ONESF[0:1, :], STRr(4), start=True, stop=True), inc="sT")
            B.wait("vector", ev)
            ev = B.emit("vector", lambda e: e.tensor_copy(SBC[:], PSTRF[:]), inc="sV")
            B.wait("tensor", ev)
            ev = B.emit("tensor", lambda e: e.matmul(
                PSTRF[:], ONESF[0:1, :], STRr(5), start=True, stop=True), inc="sT")
            B.wait("vector", ev)
            ev = B.emit("vector", lambda e: e.tensor_copy(IVBC[:], PSTRF[:]), inc="sV")
            trf_free[0] = ev

        LSET = ((UT, VT, GT), (UT2, VT2, GT2))

        lad_prev = [None, None]

        def ladder_pair(jobs, flush=False):
            n = len(jobs)
            ev_w = [None] * n
            ev_sg = [None] * n
            for i in range(n):
                B.wait("vector", lad_prev[i])
                ev_w[i] = B.emit("vector", lambda e, s_=jobs[i][0], U_=LSET[i][0]: e.tensor_tensor(
                    U_[:], s_, SBC[:], ALU.mult), inc="sV", drain=False)
            for li, c in enumerate(LADDER):
                s = 1.0 if li % 2 == 0 else -1.0
                for i in range(n):
                    wsrc = LSET[i][0] if li == 0 else LSET[i][1]
                    B.wait("scalar", ev_w[i])
                    ev_sg[i] = B.emit("scalar", lambda e, w_=wsrc, G_=LSET[i][2], s2_=s: e.activation(
                        G_[:], w_[:], AF.Sign, bias=HALFP[0:128, 0:1], scale=s2_), inc="sA")
                for i in range(n):
                    wsrc = LSET[i][0] if li == 0 else LSET[i][1]
                    B.wait("vector", ev_sg[i])
                    ev_w[i] = B.emit("vector", lambda e, w_=wsrc, V_=LSET[i][1], G_=LSET[i][2],
                                     k_=s * c / 2: e.scalar_tensor_tensor(
                        V_[:], G_[:], k_, w_[:], ALU.mult, ALU.subtract), inc="sV", drain=False)
            evt = [None] * n
            for i in range(n):
                B.wait("vector", ev_w[i])
                evt[i] = B.emit("vector", lambda e, U_=LSET[i][0], V_=LSET[i][1],
                                G_=LSET[i][2]: e.tensor_tensor(
                    G_[:], U_[:], V_[:], ALU.subtract), inc="sV", drain=False)
            ev = None
            for i in range(n):
                B.wait("vector", evt[i])
                ev = B.emit("vector", lambda e, d_=jobs[i][1], G_=LSET[i][2]: e.scalar_tensor_tensor(
                    d_, G_[:], -0.5, IVBC[:], ALU.add, ALU.mult), inc="sV",
                    drain=(flush and i == n - 1))
                lad_prev[i] = ev
            return ev

        def scale_math():
            B.emit("vector", lambda e: e.tensor_scalar(SMALL[:, 4:8], SMALL[:, 0:4], Q_EPS, None, ALU.max))
            B.emit("vector", lambda e: e.drain())
            B.emit("vector", lambda e: e.reciprocal(SMALL[:, 8:12], SMALL[:, 4:8]))
            B.emit("vector", lambda e: e.drain())
            B.emit("vector", lambda e: e.tensor_scalar(SMALL[:, 16:20], SMALL[:, 8:12], 127.0, None, ALU.mult))
            return B.emit("vector", lambda e: e.tensor_scalar(
                SMALL[:, 20:24], SMALL[:, 4:8], 1.0 / 127.0, None, ALU.mult), inc="sV")

        def phase_qx():
            remit2(lambda e: e.tensor_tensor(CM[:], Xc(0), Xc(0), ALU.max))
            remit2(lambda e: e.tensor_scalar(SBC[:], Xc(0), -1.0, None, ALU.mult))
            for kc in range(1, 16):
                remit2(lambda e, kc_=kc: e.tensor_tensor(CM[:], Xc(kc_), CM[:], ALU.max))
                evb = remit2(lambda e, kc_=kc: e.scalar_tensor_tensor(
                    SBC[:], Xc(kc_), -1.0, SBC[:], ALU.mult, ALU.max))
            B.wait("vector", evb)
            ev_cm = B.emit("vector", lambda e: e.tensor_tensor(
                CM[:], CM[:], SBC[:], ALU.max), inc="sV")
            B.wait("tensor", ev_cm)
            B.wait("tensor", trf_free[0])
            ev_t = None
            for wdx in range(4):
                ev_t = B.emit("tensor", lambda e, w_=wdx: e.transpose(
                    PSTRF[:, 128 * w_:128 * (w_ + 1)], CM[:, 128 * w_:128 * (w_ + 1)],
                    IDF[:]), inc="sT")
            B.wait("vector", ev_t)
            for wdx in range(4):
                B.emit("vector", lambda e, w_=wdx: e.tensor_reduce(
                    SMALL[:, w_:w_ + 1], PSTRF[:, 128 * w_:128 * (w_ + 1)], AX.X, ALU.max))
            ev = B.emit("vector", lambda e: e.drain(), inc="sV")
            trf_free[0] = ev
            ev_sm = scale_math()
            make_scale_bcasts(ev_sm)
            ev = None
            for kc in range(0, 16, 2):
                ev = ladder_pair([(Xc(kc), X2Qc(kc)), (Xc(kc + 1), X2Qc(kc + 1))],
                                 flush=(kc == 14))
            return ev

        def wq_halves(wt_r, c0, k0_base, scratch_r, rms_off, w):
            B.wait("vector", rmsc_ev[rms_off // 4])
            ev_q = None
            for qtr in range(4):
                B.wait("sync", wf_free[0])
                B.wait("sync", wf2_free[0])
                ev_wf = dma("sync", WF[:].rearrange("p (k n) -> p k n", n=512),
                            wt_r[:, k0_base + 4 * qtr:k0_base + 4 * qtr + 4, c0:c0 + 512],
                            inc="sF")
                B.wait("vector", ev_wf)
                if qtr == 0:
                    waitall("vector", wslot_free[w])
                for cp in (0, 2):
                    for i in range(2):
                        remit2(lambda e, c_=cp + i, r_=rms_off, U_=LSET[i][0]: e.tensor_scalar(
                            U_[:], WF[:, 512 * c_:512 * (c_ + 1)],
                            RMSC[:, r_:r_ + 1], RMSC[:, r_ + 2:r_ + 3], ALU.is_ge, ALU.mult))
                    for i in range(2):
                        remit2(lambda e, c_=cp + i, r_=rms_off, V_=LSET[i][1]: e.tensor_scalar(
                            V_[:], WF[:, 512 * c_:512 * (c_ + 1)],
                            RMSC[:, r_ + 1:r_ + 2], RMSC[:, r_ + 3:r_ + 4], ALU.is_le, ALU.mult))
                    for i in range(2):
                        last = qtr == 3 and cp == 2 and i == 1
                        ev_q = remit2(lambda e, w_=w, kcw_=4 * qtr + cp + i,
                                      U_=LSET[i][0], V_=LSET[i][1]: e.tensor_tensor(
                            Wtile(w_, kcw_), U_[:], V_[:], ALU.add), drain=last)
                wf_free[0] = ev_q
            wslot_ready[w] = ev_q
            B.wait("sync", ev_q)
            ev_s = dma("sync", scratch_r[:, k0_base:k0_base + 16, c0:c0 + 512],
                       WSLOT(w).rearrange("p (k n) -> p k n", n=512), inc="sS" + str(w))
            scratch_ev[w] = ev_s

        w1T_r = w1T.rearrange("(k p) n -> p k n", p=128)
        w2T_r = w2T.rearrange("(k p) n -> p k n", p=128)
        wq1S_r = wq1S.rearrange("(k p) n -> p k n", p=128)
        wq2S_r = wq2S.rearrange("(k p) n -> p k n", p=128)

        def ffn1(p):
            ev_h = None
            for sl in range(16):
                if p == 0:
                    w = wslot_next[0]
                    wslot_next[0] = 1 - w
                    wq_halves(w1T_r, 512 * sl, 0, wq1S_r, 0, w)
                else:
                    w = load_wslice(wq1S_r[:, :, 512 * sl:512 * (sl + 1)], cast=False)
                for j in range(4):
                    hc = 4 * sl + j
                    bi = next_bank2()
                    pb = PSB[bi]
                    B.wait("tensor", psum_free[bi])
                    B.wait("tensor", wslot_ready[w])
                    ev_mm = None
                    for kc in range(16):
                        ev_mm = B.emit("tensor", lambda e, kc_=kc, j_=j, w_=w, pb_=pb: e.matmul(
                            pb_[:], Wtile(w_, kc_)[:, 128 * j_:128 * (j_ + 1)], X2Qc(kc_),
                            start=(kc_ == 0), stop=(kc_ == 15)),
                            inc="sT" if kc == 15 else None)
                    if j == 3:
                        wslot_free[w] = [ev_mm] + ([scratch_ev[w]] if scratch_ev[w] else [])
                        scratch_ev[w] = None
                    B.wait("scalar", ev_mm)
                    ev_g = B.emit("scalar", lambda e, hc_=hc, pb_=pb: e.activation(
                        Hc(hc_), pb_[:], AF.Gelu), inc="sA")
                    psum_free[bi] = ev_g
                    B.wait("vector", ev_g)
                    if hc == 0:
                        remit2(lambda e: e.tensor_copy(CMHB[:], Hc(0)))
                        ev_h = remit2(lambda e: e.tensor_scalar(
                            CMHB2[:], Hc(0), -1.0, None, ALU.mult))
                    else:
                        remit2(lambda e, hc_=hc: e.tensor_tensor(
                            CMHB[:], Hc(hc_), CMHB[:], ALU.max))
                        ev_h = remit2(lambda e, hc_=hc: e.scalar_tensor_tensor(
                            CMHB2[:], Hc(hc_), -1.0, CMHB2[:], ALU.mult, ALU.max))
            return ev_h

        def phase_qh(ev_cmh):
            B.wait("vector", ev_cmh)
            ev_cm = B.emit("vector", lambda e: e.tensor_tensor(
                CMHB[:], CMHB[:], CMHB2[:], ALU.max), inc="sV")
            B.wait("tensor", ev_cm)
            B.wait("tensor", trb_free[0])
            ev_t = None
            for wdx in range(4):
                ev_t = B.emit("tensor", lambda e, w_=wdx: e.transpose(
                    PSTRB[:, 128 * w_:128 * (w_ + 1)], CMHB[:, 128 * w_:128 * (w_ + 1)],
                    IDB[:]), inc="sT")
            B.wait("vector", ev_t)
            for wdx in range(4):
                B.emit("vector", lambda e, w_=wdx: e.tensor_reduce(
                    SMALL[:, w_:w_ + 1], PSTRB[:, 128 * w_:128 * (w_ + 1)], AX.X, ALU.max))
            ev = B.emit("vector", lambda e: e.drain(), inc="sV")
            trb_free[0] = ev
            ev_sm = scale_math()
            make_scale_bcasts(ev_sm)
            ev = None
            for hc in range(0, NHC, 2):
                ev = ladder_pair([(Hc(hc), Hc(hc)), (Hc(hc + 1), Hc(hc + 1))],
                                 flush=(hc == NHC - 2))
            return ev

        def ffn2(p):
            ev_stats = None
            for og in range(4):
                ev_mm = None
                for qt in range(4):
                    if p == 0:
                        w = wslot_next[0]
                        wslot_next[0] = 1 - w
                        wq_halves(w2T_r, 512 * og, 16 * qt, wq2S_r, 4, w)
                    else:
                        w = load_wslice(
                            wq2S_r[:, 16 * qt:16 * (qt + 1), 512 * og:512 * (og + 1)],
                            cast=False)
                    if qt == 0:
                        for bi in range(4):
                            B.wait("tensor", psum_free[bi])
                    B.wait("tensor", wslot_ready[w])
                    for kc in range(16):
                        for j in range(4):
                            last = (kc == 15 and j == 3)
                            ev_mm = B.emit("tensor", lambda e, kc_=kc, j_=j, qt_=qt, w_=w: e.matmul(
                                PSB[j_][:], Wtile(w_, kc_)[:, 128 * j_:128 * (j_ + 1)],
                                Hc(16 * qt_ + kc_),
                                start=(qt_ == 0 and kc_ == 0), stop=(qt_ == 3 and kc_ == 15)),
                                inc="sT" if last else None)
                    wslot_free[w] = [ev_mm] + ([scratch_ev[w]] if scratch_ev[w] else [])
                    scratch_ev[w] = None
                B.wait("vector", ev_mm)
                for j in range(4):
                    och = 4 * og + j
                    B.wait("vector", rbsq_free[0])
                    ev_r = B.emit("vector", lambda e, j_=j, och_=och: e.tensor_tensor(
                        Xc(och_), PSB[j_][:], Xc(och_), ALU.add), inc="sV", drain=False)
                    psum_free[j] = ev_r
                    B.wait("vector", ev_r)
                    B.emit("vector", lambda e, och_=och: e.tensor_copy(RB[:], Xc(och_)))
                    ev_sqb = B.emit("vector", lambda e, och_=och: e.tensor_tensor(
                        SQB[:], Xc(och_), Xc(och_), ALU.mult), inc="sV")
                    B.wait("tensor", ev_sqb)
                    B.wait("tensor", strips_free[0])
                    first = (og == 0 and j == 0)
                    lastt = (og == 3 and j == 3)
                    B.emit("tensor", lambda e, f_=first, l_=lastt: e.matmul(
                        PSSUM[:], ONE_COL_B, RB[:], start=f_, stop=l_))
                    ev_stats = B.emit("tensor", lambda e, f_=first, l_=lastt: e.matmul(
                        PSSQ[:], ONE_COL_B, SQB[:], start=f_, stop=l_), inc="sT")
                    rbsq_free[0] = ev_stats
            return ev_stats

        def phase_out(p, ev_ln2):
            B.wait("sync", ev_ln2)
            return dma("sync", outT.rearrange("(k p) n -> p k n", p=128)[:, :, TP * p:TP * (p + 1)],
                       X.rearrange("p (k n) -> p k n", n=512), inc="sO")

        out_ev = None
        for p in range(npass):
            hook["fn"] = (lambda: prepass_pump(8)) if p == 0 else None
            xb_ev = phase_x_load(p, out_ev)
            pump_hook()
            pump_hook()
            B.barrier()
            if p == 0:
                _dbg("xb", XBfull)
            phase_beta(xb_ev)
            pump_hook()
            B.barrier()
            if p == 0:
                _dbg("beta", BETA[:])
            phase_qk(wqT, qb, QNr)
            B.barrier()
            if p == 0:
                _dbg("qn", BIGA[:, 0:8192])
            phase_qk(wkT, kb, KNr)
            phase_v()
            B.barrier()
            if p == 0:
                _dbg("kn", BIGA[:, 8192:16384])
                _dbg("gv", BIGA[:, 16384:24576])
            phase_og()
            B.barrier()
            if p == 0:
                _dbg("gate", BIGA[:, 24576:32768])
            phase_att()
            B.barrier()
            if p == 0:
                _dbg("u", BIGA[:, 24576:32768])
            ev_stats = phase_oproj()
            B.barrier()
            if p == 0:
                _dbg("r1", X[:])
            phase_ln(ev_stats)
            pump_hook()
            B.barrier()
            if p == 0:
                _dbg("x2", X[:])
            phase_qx()
            if p == 0:
                prepass_pump(256)
            B.barrier()
            if p == 0:
                _dbg("x2q", BIGA[:, 32768:40960])
                _dbg("sbc", SBC[:])
                _dbg("ivbc", IVBC[:])
            ev_cmh = ffn1(p)
            B.barrier()
            if p == 0:
                _dbg("h", BIGA[:, 0:32768])
                _dbg("cmhb", CMHB[:])
            phase_qh(ev_cmh)
            B.barrier()
            if p == 0:
                _dbg("hq", BIGA[:, 0:32768])
            ev_stats2 = ffn2(p)
            B.barrier()
            if p == 0:
                _dbg("r2", X[:])
            if p == 0 and "wq2s" in DEBUG:
                t1 = nc.dram_tensor("dbg_wq1s", [D, DF], BF16, kind="ExternalOutput").ap()
                t2 = nc.dram_tensor("dbg_wq2s", [DF, D], BF16, kind="ExternalOutput").ap()
                B.barrier()
                ev1 = dma("sync", t1[:, :], wq1S[:, :], inc="sO")
                ev2 = dma("sync", t2[:, :], wq2S[:, :], inc="sO")
                for eng in ENGINES:
                    B.wait(eng, ev2)
            ev_ln2 = phase_ln(ev_stats2)
            out_ev = phase_out(p, ev_ln2)
            B.barrier()
        for name in list(B.cnt):
            if name.startswith(("sO", "sS")):
                B.wait("sync", B.now(name))

        with contextlib.ExitStack() as stack:
            sems = {name: stack.enter_context(nc.semaphore("sem_" + name)) for name in B.cnt}
            with nc.Block() as block:
                B.replay(block, sems)

    return nc


def _host_prep(inputs):
    f32 = np.float32
    x = np.asarray(inputs["L0_attn_hidden_states"], f32)[0]
    xT = np.ascontiguousarray(x.T)
    com = {
        "wqT": np.ascontiguousarray(np.asarray(inputs["L0_attn_q_w"], f32).T),
        "wkT": np.ascontiguousarray(np.asarray(inputs["L0_attn_k_w"], f32).T),
        "wvT": np.ascontiguousarray(np.asarray(inputs["L0_attn_v_w"], f32).T),
        "wogT": np.ascontiguousarray(np.asarray(inputs["L0_attn_og_w"], f32).T),
        "woT": np.ascontiguousarray(np.asarray(inputs["L0_attn_o_w"], f32).T),
        "w1T": np.ascontiguousarray(np.asarray(inputs["L0_ffn_w1"], f32).T),
        "w2T": np.ascontiguousarray(np.asarray(inputs["L0_ffn_w2"], f32).T),
        "bwT": np.ascontiguousarray(np.asarray(inputs["L0_attn_b_w"], f32).T),
        "qb": np.ascontiguousarray(np.asarray(inputs["L0_attn_q_b"], f32)[None, :]),
        "kb": np.ascontiguousarray(np.asarray(inputs["L0_attn_k_b"], f32)[None, :]),
        "vb": np.ascontiguousarray(np.asarray(inputs["L0_attn_v_b"], f32)[None, :]),
        "ob": np.ascontiguousarray(np.asarray(inputs["L0_attn_o_b"], f32)[None, :]),
        "bb": np.ascontiguousarray(np.asarray(inputs["L0_attn_b_b"], f32)[None, :]),
    }
    com["idb"] = np.eye(128, dtype=ml_dtypes.bfloat16)
    com["idf"] = np.eye(128, dtype=f32)
    pos = np.arange(64, dtype=f32)
    dist = np.clip(pos[None, :] - pos[:, None], 0.0, None)
    decay = np.exp(-0.1 * dist).astype(f32)
    causal = pos[None, :] <= pos[:, None]
    m64 = np.where(causal, decay, 0.0).astype(f32)
    m128 = np.zeros((128, 128), f32)
    m128[0:64, 0:64] = m64.T
    m128[64:128, 64:128] = m64.T
    com["maskf"] = m128
    com["onesb"] = np.ones((128, 512), dtype=ml_dtypes.bfloat16)
    com["onesf"] = np.ones((8, 128), dtype=f32)
    com["onecf"] = np.ones((128, 1), dtype=f32)
    return xT, com


def kernel(**inputs):
    xT, com = _host_prep(inputs)
    L = xT.shape[1]
    tpc = L // NCORES
    nc = build_program(tpc)
    in_maps = []
    for c in range(NCORES):
        m = dict(com)
        m["xT"] = np.ascontiguousarray(xT[:, c * tpc:(c + 1) * tpc])
        in_maps.append(m)
    res = run_bass_kernel_spmd(nc, in_maps, core_ids=list(range(NCORES)))
    parts = [np.asarray(r["outT"]).T for r in res.results]
    out = np.concatenate(parts, axis=0)[None, :, :]
    return np.ascontiguousarray(out, dtype=np.float32)


# version 11
# speedup vs baseline: 1.0156x; 1.0091x over previous
import contextlib
import numpy as np
import ml_dtypes
import concourse.bass as bass
import concourse.mybir as mybir
from concourse.bass_utils import run_bass_kernel_spmd

F32 = mybir.dt.float32
BF16 = mybir.dt.bfloat16
AF = mybir.ActivationFunctionType
ALU = mybir.AluOpType
AX = mybir.AxisListType

NCORES = 8
D = 2048
DF = 8192
NH = 16
TP = 512
NKC = D // 128
NHC = DF // 128
EPS_L2 = 1.0e-12
SCALE = float(128) ** 0.5
E2_L2 = (SCALE * EPS_L2) ** 2
LN_EPS = 1.0e-5
Q_EPS = 1.0e-5
LADDER = [128.0, 64.0, 32.0, 16.0, 8.0, 4.0, 2.0, 1.0]
ENGINES = ("sync", "vector", "scalar", "tensor", "gpsimd")
DEBUG = ()
SAFE_QK = False
ACT_FUDGE = 0


class Builder:
    def __init__(self):
        self.prog = {e: [] for e in ENGINES}
        self.cnt = {}
        self.last_ev = {}

    def _ensure(self, name):
        if name not in self.cnt:
            self.cnt[name] = 0

    def wait(self, eng, ev):
        if ev is None:
            return
        name, val = ev
        if val <= 0:
            return
        self._ensure(name)
        self.prog[eng].append(("w", name, val))

    def emit(self, eng, fn, inc=None, k=1, drain=True):
        if inc is None:
            self.prog[eng].append(("d", fn))
            return None
        self._ensure(inc)
        self.cnt[inc] += k
        if eng in ("vector", "scalar") and drain:
            self.prog[eng].append(("d", fn))
            self.prog[eng].append(("i", lambda e: e.drain(), inc, k))
        else:
            self.prog[eng].append(("i", fn, inc, k))
        if not inc.endswith("p"):
            self.last_ev[eng] = (inc, self.cnt[inc])
        return (inc, self.cnt[inc])

    def now(self, name):
        self._ensure(name)
        return (name, self.cnt[name])

    def barrier(self, engines=ENGINES):
        evs = list(self.last_ev.values())
        for eng in engines:
            for ev in evs:
                self.wait(eng, ev)

    def replay(self, block, sems):
        def make(eng):
            def body(e):
                for it in self.prog[eng]:
                    if it[0] == "w":
                        e.wait_ge(sems[it[1]], it[2])
                    elif it[0] == "d":
                        it[1](e)
                    else:
                        it[1](e).then_inc(sems[it[2]], it[3])
            return body
        for eng in ENGINES:
            getattr(block, eng)(make(eng))


def build_program(tpc):
    npass = tpc // TP
    nc = bass.Bass("TRN2", target_bir_lowering=False)

    xT = nc.dram_tensor("xT", [D, tpc], F32, kind="ExternalInput").ap()
    wqT = nc.dram_tensor("wqT", [D, D], F32, kind="ExternalInput").ap()
    wkT = nc.dram_tensor("wkT", [D, D], F32, kind="ExternalInput").ap()
    wvT = nc.dram_tensor("wvT", [D, D], F32, kind="ExternalInput").ap()
    wogT = nc.dram_tensor("wogT", [D, D], F32, kind="ExternalInput").ap()
    woT = nc.dram_tensor("woT", [D, D], F32, kind="ExternalInput").ap()
    w1T = nc.dram_tensor("w1T", [D, DF], F32, kind="ExternalInput").ap()
    w2T = nc.dram_tensor("w2T", [DF, D], F32, kind="ExternalInput").ap()
    bwT = nc.dram_tensor("bwT", [D, NH], F32, kind="ExternalInput").ap()
    qb = nc.dram_tensor("qb", [1, D], F32, kind="ExternalInput").ap()
    kb = nc.dram_tensor("kb", [1, D], F32, kind="ExternalInput").ap()
    vb = nc.dram_tensor("vb", [1, D], F32, kind="ExternalInput").ap()
    ob = nc.dram_tensor("ob", [1, D], F32, kind="ExternalInput").ap()
    bb = nc.dram_tensor("bb", [1, NH], F32, kind="ExternalInput").ap()
    idb = nc.dram_tensor("idb", [128, 128], BF16, kind="ExternalInput").ap()
    idf = nc.dram_tensor("idf", [128, 128], F32, kind="ExternalInput").ap()
    maskf = nc.dram_tensor("maskf", [128, 128], F32, kind="ExternalInput").ap()
    onesb = nc.dram_tensor("onesb", [128, 512], BF16, kind="ExternalInput").ap()
    onesf = nc.dram_tensor("onesf", [8, 128], F32, kind="ExternalInput").ap()
    onecf = nc.dram_tensor("onecf", [128, 1], F32, kind="ExternalInput").ap()
    outT = nc.dram_tensor("outT", [D, tpc], F32, kind="ExternalOutput").ap()
    wq1S = nc.dram_tensor("wq1S", [D, DF], BF16, kind="Internal").ap()
    wq2S = nc.dram_tensor("wq2S", [DF, D], BF16, kind="Internal").ap()

    B = Builder()

    _st = contextlib.ExitStack()
    with _st:
        X = _st.enter_context(nc.sbuf_tensor("Xs", [128, NKC * TP], F32))
        BIGA = _st.enter_context(nc.sbuf_tensor("BIGAs", [128, 40960], BF16))
        WB = _st.enter_context(nc.sbuf_tensor("WBs", [128, 16384], BF16))
        WF = _st.enter_context(nc.sbuf_tensor("WFs", [128, 2048], F32))
        BETA = _st.enter_context(nc.sbuf_tensor("BETAs", [128, 64], F32))
        UT = _st.enter_context(nc.sbuf_tensor("UTs", [128, 512], F32))
        VT = _st.enter_context(nc.sbuf_tensor("VTs", [128, 512], F32))
        GT = _st.enter_context(nc.sbuf_tensor("GTs", [128, 512], F32))
        SBC = _st.enter_context(nc.sbuf_tensor("SBCs", [128, 512], F32))
        IVBC = _st.enter_context(nc.sbuf_tensor("IVBCs", [128, 512], F32))
        QTMB = _st.enter_context(nc.sbuf_tensor("QTMBs", [128, 512], BF16))
        RB = _st.enter_context(nc.sbuf_tensor("RBs", [128, 512], BF16))
        SQB = _st.enter_context(nc.sbuf_tensor("SQBs", [128, 512], BF16))
        CMHB = _st.enter_context(nc.sbuf_tensor("CMHBs", [128, 512], BF16))
        MSA = _st.enter_context(nc.sbuf_tensor("MSAs", [128, 128], BF16))
        MSB = _st.enter_context(nc.sbuf_tensor("MSBs", [128, 128], BF16))
        SMALL = _st.enter_context(nc.sbuf_tensor("SMALLs", [128, 64], F32))
        RMSC = _st.enter_context(nc.sbuf_tensor("RMSCs", [128, 8], F32))
        T8 = _st.enter_context(nc.sbuf_tensor("T8s", [8, 128], F32))
        UT2 = _st.enter_context(nc.sbuf_tensor("UT2s", [128, 512], F32))
        VT2 = _st.enter_context(nc.sbuf_tensor("VT2s", [128, 512], F32))
        GT2 = _st.enter_context(nc.sbuf_tensor("GT2s", [128, 512], F32))
        CMHB2 = _st.enter_context(nc.sbuf_tensor("CMHB2s", [128, 512], BF16))
        SQF2 = _st.enter_context(nc.sbuf_tensor("SQF2s", [128, 1024], F32))
        NEGH = _st.enter_context(nc.sbuf_tensor("NEGHs", [128, 4], F32))
        HALFP = _st.enter_context(nc.sbuf_tensor("HALFPs", [128, 1], F32))
        STR = _st.enter_context(nc.sbuf_tensor("STRs", [1, 3072], F32))
        SM2 = _st.enter_context(nc.sbuf_tensor("SM2s", [1, 64], F32))
        IDB = _st.enter_context(nc.sbuf_tensor("IDBs", [128, 128], BF16))
        IDF = _st.enter_context(nc.sbuf_tensor("IDFs", [128, 128], F32))
        MASK = _st.enter_context(nc.sbuf_tensor("MASKs", [128, 128], F32))
        ONESB = _st.enter_context(nc.sbuf_tensor("ONESBs", [128, 512], BF16))
        ONESF = _st.enter_context(nc.sbuf_tensor("ONESFs", [8, 128], F32))
        ONECF = _st.enter_context(nc.sbuf_tensor("ONECFs", [128, 1], F32))
        SM3 = _st.enter_context(nc.sbuf_tensor("SM3s", [1, 16], F32))
        HALFC = _st.enter_context(nc.sbuf_tensor("HALFCs", [1, 1], F32))
        BWTB = _st.enter_context(nc.sbuf_tensor("BWTBs", [128, 256], BF16))
        BROW = _st.enter_context(nc.sbuf_tensor("BROWs", [1, D], BF16))
        BBB = _st.enter_context(nc.sbuf_tensor("BBBs", [1, NH], BF16))
        PS0 = _st.enter_context(nc.psum_tensor("PS0", [128, 512], F32))
        PS1 = _st.enter_context(nc.psum_tensor("PS1", [128, 512], F32))
        PS2 = _st.enter_context(nc.psum_tensor("PS2", [128, 512], F32))
        PS3 = _st.enter_context(nc.psum_tensor("PS3", [128, 512], F32))
        PSSUM = _st.enter_context(nc.psum_tensor("PSSUM", [1, 512], F32))
        PSSQ = _st.enter_context(nc.psum_tensor("PSSQ", [1, 512], F32))
        PSTRB = _st.enter_context(nc.psum_tensor("PSTRB", [128, 512], BF16))
        PSTRF = _st.enter_context(nc.psum_tensor("PSTRF", [128, 512], F32))
        PSB = [PS0, PS1, PS2, PS3]
        CM = GT
        SQS = UT

        QNr = BIGA[:, 0:8192].rearrange("p (h n) -> p h n", n=512)
        KNr = BIGA[:, 8192:16384].rearrange("p (h n) -> p h n", n=512)

        def QN(h):
            return BIGA[:, 512 * h:512 * (h + 1)]

        def KN(h):
            return BIGA[:, 8192 + 512 * h:8192 + 512 * (h + 1)]

        def GVT(tt):
            return BIGA[:, 16384 + 2048 * tt:16384 + 2048 * (tt + 1)]

        def USL(h):
            return BIGA[:, 24576 + 512 * h:24576 + 512 * (h + 1)]

        def XBc(kc):
            return BIGA[:, 32768 + 512 * kc:32768 + 512 * (kc + 1)]

        XBfull = BIGA[:, 32768:40960]
        X2Qc = XBc

        def Hc(hc):
            return BIGA[:, 512 * hc:512 * (hc + 1)]

        def Xc(kc):
            return X[:, 512 * kc:512 * (kc + 1)]

        def WSLOT(w):
            return WB[:, 8192 * w:8192 * (w + 1)]

        def Wtile(w, kc):
            return WB[:, 8192 * w + 512 * kc:8192 * w + 512 * (kc + 1)]

        def STRr(r):
            return STR[0:1, 512 * r:512 * (r + 1)]

        psum_free = {i: None for i in range(4)}
        strips_free = [None]
        trb_free = [None]
        trf_free = [None]
        wslot_free = {0: [], 1: []}
        scratch_ev = {0: None, 1: None}
        wslot_ready = {0: None, 1: None}
        wslot_next = [0]
        wf_free = [None]
        wf2_free = [None]
        sqs_free = {0: None, 1: None}
        qtmb_free = {0: None, 1: None}
        rbsq_free = [None]
        ms_free = {0: None, 1: None}
        bank_i = [0]
        rv_hist = [None, None]
        brow_free = [None]

        def next_bank2():
            b = bank_i[0] % 4
            bank_i[0] += 1
            return b

        def remit2(fn, drain=False):
            B.wait("vector", rv_hist[0])
            ev = B.emit("vector", fn, inc="sV", drain=drain)
            rv_hist[0], rv_hist[1] = rv_hist[1], ev
            return ev

        def waitall(eng, evs):
            for ev in evs:
                B.wait(eng, ev)

        gdma_last = {}

        def dma(eng, out_ap, in_ap, inc="sD"):
            if eng == "gpsimd":
                inc = inc + "g"
                B.wait("gpsimd", gdma_last.get(inc))
            ev = B.emit(eng, lambda e, o=out_ap, i=in_ap: e.dma_start(out=o, in_=i), inc=inc, k=16)
            if eng == "gpsimd":
                gdma_last[inc] = ev
            return ev

        def cs(wt, c0, cw, k0=None, kn=None):
            r = wt.rearrange("(k p) n -> p k n", p=128)
            if k0 is None:
                return r[:, :, c0:c0 + cw]
            return r[:, k0:k0 + kn, c0:c0 + cw]

        def load_wslice(in_ap, cast, colw=512):
            w = wslot_next[0]
            wslot_next[0] = 1 - w
            eng = "gpsimd" if cast else "sync"
            waitall(eng, wslot_free[w])
            ev = dma(eng, WSLOT(w).rearrange("p (k n) -> p k n", n=colw), in_ap, inc="sW" + str(w))
            wslot_ready[w] = ev
            return w

        def _dbg(name, ap):
            if name not in DEBUG:
                return
            sh = list(ap.shape)
            t = nc.dram_tensor("dbg_" + name, sh, ap.dtype, kind="ExternalOutput").ap()
            B.barrier()
            idx = tuple(slice(0, s) for s in sh)
            ev = dma("sync", t[idx], ap, inc="sO")
            for eng in ENGINES:
                B.wait(eng, ev)

        dma("sync", IDB[:], idb[:, :], inc="sC")
        dma("sync", IDF[:], idf[:, :], inc="sC")
        dma("sync", MASK[:], maskf[:, :], inc="sC")
        dma("sync", ONESB[:], onesb[:, :], inc="sC")
        dma("sync", ONESF[:], onesf[:, :], inc="sC")
        dma("sync", ONECF[:], onecf[:, :], inc="sC")
        dma("gpsimd", BBB[:], bb[:, :], inc="sC")
        dma("gpsimd", BWTB[:].rearrange("p (k n) -> p k n", n=NH),
            bwT.rearrange("(k p) n -> p k n", p=128), inc="sC")
        B.emit("gpsimd", lambda e: e.memset(NEGH[:], -0.5), inc="sC", k=1)
        B.emit("gpsimd", lambda e: e.memset(HALFC[:], 0.5), inc="sC", k=1)
        B.emit("gpsimd", lambda e: e.memset(HALFP[:], 0.5), inc="sC", k=1)
        for ev in (B.now("sC"), B.now("sCg")):
            for eng in ENGINES:
                B.wait(eng, ev)

        ONE_ROW_B = ONESB[0:1, :]
        ONE_COL_B = ONESB[:, 0:1]

        pp_quarters = []
        for m, nkq in ((0, 8), (1, 32)):
            for kq in range(nkq):
                for cq in range(128 // nkq):
                    pp_quarters.append((m, kq, cq))
        w_r = {0: w1T.rearrange("(k p) n -> p k n", p=128),
               1: w2T.rearrange("(k p) n -> p k n", p=128)}
        pp = {"pos": 0, "qidx": 0, "ev_mm": None}
        rmsc_ev = [None, None]

        def pp_finalize(m):
            B.emit("vector", lambda e: e.tensor_reduce(
                SM3[0:1, 8:10], SM2[0:1, 0:16].rearrange("p (s n) -> p s n", n=8), AX.X, ALU.add))
            B.emit("vector", lambda e: e.drain())
            B.emit("vector", lambda e: e.tensor_reduce(SM3[0:1, 0:1], SM3[0:1, 8:10], AX.X, ALU.add))
            B.emit("vector", lambda e: e.drain())
            ev = B.emit("vector", lambda e: e.tensor_scalar(
                SM3[0:1, 1:2], SM3[0:1, 0:1], 1.0 / float(D * DF), Q_EPS, ALU.mult, ALU.max),
                inc="sVp")
            B.wait("gpsimd", ev)
            ev = B.emit("gpsimd", lambda e: e.tensor_tensor(
                SM3[0:1, 2:3], SM3[0:1, 1:2], HALFC[:], ALU.pow), inc="sGp")
            B.wait("vector", ev)
            B.emit("vector", lambda e: e.tensor_scalar(SM3[0:1, 4:5], SM3[0:1, 2:3], 0.5, None, ALU.mult))
            B.emit("vector", lambda e: e.tensor_scalar(SM3[0:1, 5:6], SM3[0:1, 2:3], -0.5, None, ALU.mult))
            B.emit("vector", lambda e: e.tensor_scalar(SM3[0:1, 6:7], SM3[0:1, 2:3], 1.0, None, ALU.mult))
            ev = B.emit("vector", lambda e: e.tensor_scalar(
                SM3[0:1, 7:8], SM3[0:1, 2:3], -1.0, None, ALU.mult), inc="sVp")
            B.wait("tensor", ev)
            B.wait("tensor", trf_free[0])
            ev = B.emit("tensor", lambda e: e.matmul(
                PSTRF[:, 0:4], ONESF[0:1, :], SM3[0:1, 4:8], start=True, stop=True), inc="sTp")
            B.wait("vector", ev)
            ev = B.emit("vector", lambda e, m_=m: e.tensor_copy(
                RMSC[:, 4 * m_:4 * m_ + 4], PSTRF[:, 0:4]), inc="sVp")
            trf_free[0] = ev
            rmsc_ev[m] = ev

        def prepass_pump(n):
            assert n % 8 == 0
            for _ in range(n):
                if pp["pos"] >= len(pp_quarters):
                    return
                m, kq, cq = pp_quarters[pp["pos"]]
                pp["pos"] += 1
                odd = pp["qidx"] % 2
                buf = WF[:, 1024:2048] if odd else WF[:, 0:1024]
                guard = wf2_free if odd else wf_free
                B.wait("sync", guard[0])
                ev_wf = dma("sync", buf.rearrange("p (k n) -> p k n", n=512),
                            w_r[m][:, 2 * kq:2 * kq + 2, 512 * cq:512 * (cq + 1)],
                            inc="sFBp" if odd else "sFp")
                B.wait("scalar", ev_wf)
                B.wait("scalar", pp["ev_mm"])
                ev_sq = B.emit("scalar", lambda e, b_=buf: e.activation(SQF2[:], b_, AF.Square),
                               inc="sAp")
                guard[0] = ev_sq
                B.wait("tensor", ev_sq)
                grp = pp["qidx"] % 8
                if grp == 0:
                    B.wait("tensor", strips_free[0])
                ev = None
                for c4 in range(2):
                    ev = B.emit("tensor", lambda e, c_=c4, g_=grp: e.matmul(
                        PSSUM[:], ONECF[:], SQF2[:, 512 * c_:512 * (c_ + 1)],
                        start=(g_ == 0 and c_ == 0), stop=(g_ == 7 and c_ == 1)), inc="sTp")
                pp["ev_mm"] = ev
                if grp == 7:
                    B.wait("vector", ev)
                    B.emit("vector", lambda e: e.tensor_reduce(
                        SM3[0:1, 8:12], PSSUM[0:1, :].rearrange("p (s n) -> p s n", n=128),
                        AX.X, ALU.add))
                    B.emit("vector", lambda e: e.drain())
                    ev_str = B.emit("vector", lambda e, s_=(pp["qidx"] // 8) % 16: e.tensor_reduce(
                        SM2[0:1, s_:s_ + 1], SM3[0:1, 8:12], AX.X, ALU.add), inc="sVp")
                    strips_free[0] = ev_str
                pp["qidx"] += 1
                if pp["qidx"] in (128, 256):
                    pp_finalize(0 if pp["qidx"] == 128 else 1)
        hook = {"fn": None}

        def pump_hook():
            if hook["fn"]:
                hook["fn"]()

        def phase_x_load(p, prev_out_ev):
            B.wait("sync", prev_out_ev)
            ev = dma("sync", X.rearrange("p (k n) -> p k n", n=512),
                     xT.rearrange("(k p) n -> p k n", p=128)[:, :, TP * p:TP * (p + 1)], inc="sX")
            B.wait("vector", ev)
            return B.emit("vector", lambda e: e.tensor_copy(XBfull, X[:]), inc="sV")

        def phase_beta(xb_ev):
            B.wait("tensor", xb_ev)
            ev = None
            for tt in range(4):
                for kc in range(16):
                    B.emit("tensor", lambda e, tt_=tt, kc_=kc: e.matmul(
                        PS2[:, 16 * tt_:16 * (tt_ + 1)],
                        XBc(kc_)[:, 128 * tt_:128 * (tt_ + 1)],
                        BWTB[:, NH * kc_:NH * (kc_ + 1)],
                        start=(kc_ == 0), stop=False))
                ev = B.emit("tensor", lambda e, tt_=tt: e.matmul(
                    PS2[:, 16 * tt_:16 * (tt_ + 1)], ONE_ROW_B[:, 0:128], BBB[0:1, :],
                    start=False, stop=True), inc="sT")
            B.wait("scalar", ev)
            B.emit("scalar", lambda e: e.activation(SMALL[:, 0:64], PS2[:, 0:64], AF.Abs), inc="sA")
            B.emit("scalar", lambda e: e.activation(SMALL[:, 0:64], SMALL[:, 0:64], AF.Exp, scale=-1.0))
            ev = B.emit("scalar", lambda e: e.activation(BETA[:], PS2[:, 0:64], AF.Relu), inc="sA")
            psum_free[2] = ev
            B.wait("vector", ev)
            ev = B.emit("vector", lambda e: e.tensor_scalar(
                SMALL[:, 0:64], SMALL[:, 0:64], 1.0, None, ALU.add), inc="sV")
            B.wait("scalar", ev)
            ev = B.emit("scalar", lambda e: e.activation(SMALL[:, 0:64], SMALL[:, 0:64], AF.Ln), inc="sA")
            B.wait("vector", ev)
            return B.emit("vector", lambda e: e.tensor_tensor(
                BETA[:], BETA[:], SMALL[:, 0:64], ALU.add), inc="sV")

        def phase_qk(wt, bias_dram, dest_r):
            B.wait("gpsimd", brow_free[0])
            ev_b = dma("gpsimd", BROW[:], bias_dram[:, :], inc="sC")
            B.wait("tensor", ev_b)
            ev_last = None
            for s in range(4):
                pump_hook()
                w = load_wslice(cs(wt, 512 * s, 512), cast=True)
                for tt in range(4):
                    bi = next_bank2()
                    pb = PSB[bi]
                    g = 16 * bi
                    SQ = UT if bi % 2 == 0 else VT
                    QT = QTMB if bi % 2 == 0 else RB
                    B.wait("tensor", psum_free[bi])
                    B.wait("tensor", wslot_ready[w])
                    for kc in range(16):
                        B.emit("tensor", lambda e, kc_=kc, tt_=tt, w_=w, pb_=pb: e.matmul(
                            pb_[:], XBc(kc_)[:, 128 * tt_:128 * (tt_ + 1)], Wtile(w_, kc_),
                            start=(kc_ == 0), stop=False))
                    ev_mm = B.emit("tensor", lambda e, pb_=pb, s_=s: e.matmul(
                        pb_[:], ONE_ROW_B[:, 0:128], BROW[0:1, 512 * s_:512 * (s_ + 1)],
                        start=False, stop=True), inc="sT")
                    ev_last = ev_mm
                    if tt == 3:
                        wslot_free[w] = [ev_mm]
                    B.wait("scalar", ev_mm)
                    B.wait("scalar", sqs_free[bi % 2])
                    ev_sq = B.emit("scalar", lambda e, pb_=pb, SQ_=SQ: e.activation(
                        SQ_[:], pb_[:], AF.Square), inc="sA")
                    B.wait("vector", ev_sq)
                    B.emit("vector", lambda e, SQ_=SQ, g_=g: e.tensor_reduce(
                        SMALL[:, g_:g_ + 4], SQ_[:].rearrange("p (s n) -> p s n", n=128),
                        AX.X, ALU.add))
                    ev_red = B.emit("vector", lambda e: e.drain(), inc="sV")
                    sqs_free[bi % 2] = ev_red
                    ev_max = B.emit("vector", lambda e, g_=g: e.tensor_scalar(
                        SMALL[:, g_ + 4:g_ + 8], SMALL[:, g_:g_ + 4], E2_L2, None, ALU.max), inc="sV")
                    B.wait("gpsimd", ev_max)
                    ev_r = B.emit("gpsimd", lambda e, g_=g: e.tensor_tensor(
                        SMALL[:, g_ + 8:g_ + 12], SMALL[:, g_ + 4:g_ + 8], NEGH[:], ALU.pow), inc="sG")
                    B.wait("vector", ev_r)
                    B.wait("vector", qtmb_free[bi % 2])
                    ev_q = None
                    for j in range(4):
                        ev_q = B.emit("vector", lambda e, j_=j, pb_=pb, g_=g, QT_=QT: e.tensor_scalar(
                            QT_[:, 128 * j_:128 * (j_ + 1)], pb_[:, 128 * j_:128 * (j_ + 1)],
                            SMALL[:, g_ + 8 + j_:g_ + 9 + j_], None, ALU.mult),
                            inc="sV" if j == 3 else None)
                    psum_free[bi] = ev_q
                    B.wait("tensor", ev_q)
                    B.wait("tensor", trb_free[0])
                    ev_t = None
                    for j in range(4):
                        ev_t = B.emit("tensor", lambda e, j_=j, QT_=QT: e.transpose(
                            PSTRB[:, 128 * j_:128 * (j_ + 1)], QT_[:, 128 * j_:128 * (j_ + 1)],
                            IDB[:]), inc="sT")
                    qtmb_free[bi % 2] = ev_t
                    B.wait("vector", ev_t)
                    ev_e = B.emit("vector", lambda e, s_=s, tt_=tt, dr_=dest_r: e.tensor_copy(
                        dr_[:, 4 * s_:4 * s_ + 4, 128 * tt_:128 * (tt_ + 1)],
                        PSTRB[:].rearrange("p (j n) -> p j n", n=128)), inc="sV",
                        drain=(s == 3 and tt == 3))
                    trb_free[0] = ev_e
            brow_free[0] = ev_last

        def phase_v():
            B.wait("gpsimd", brow_free[0])
            ev_b = dma("gpsimd", BROW[:], vb[:, :], inc="sC")
            B.wait("tensor", ev_b)
            ev_last = None
            for s in range(4):
                pump_hook()
                w = load_wslice(cs(wvT, 512 * s, 512), cast=True)
                for tt in range(4):
                    bi = next_bank2()
                    pb = PSB[bi]
                    B.wait("tensor", psum_free[bi])
                    B.wait("tensor", wslot_ready[w])
                    for kc in range(16):
                        B.emit("tensor", lambda e, kc_=kc, tt_=tt, w_=w, pb_=pb: e.matmul(
                            pb_[:], XBc(kc_)[:, 128 * tt_:128 * (tt_ + 1)], Wtile(w_, kc_),
                            start=(kc_ == 0), stop=False))
                    ev_mm = B.emit("tensor", lambda e, pb_=pb, s_=s: e.matmul(
                        pb_[:], ONE_ROW_B[:, 0:128], BROW[0:1, 512 * s_:512 * (s_ + 1)],
                        start=False, stop=True), inc="sT")
                    ev_last = ev_mm
                    if tt == 3:
                        wslot_free[w] = [ev_mm]
                    B.wait("vector", ev_mm)
                    ev_gv = None
                    for j in range(4):
                        h = 4 * s + j
                        ev_gv = B.emit("vector", lambda e, j_=j, tt_=tt, h_=h, pb_=pb: e.tensor_scalar(
                            GVT(tt_)[:, 128 * h_:128 * (h_ + 1)],
                            pb_[:, 128 * j_:128 * (j_ + 1)],
                            BETA[:, 16 * tt_ + h_:16 * tt_ + h_ + 1], None, ALU.mult),
                            inc="sV" if j == 3 else None,
                            drain=(s == 3 and tt == 3))
                    psum_free[bi] = ev_gv
            brow_free[0] = ev_last

        def phase_og():
            for s in range(4):
                pump_hook()
                w = load_wslice(cs(wogT, 512 * s, 512), cast=True)
                for j in range(4):
                    h = 4 * s + j
                    bi = next_bank2()
                    pb = PSB[bi]
                    B.wait("tensor", psum_free[bi])
                    B.wait("tensor", wslot_ready[w])
                    ev_mm = None
                    for kc in range(16):
                        ev_mm = B.emit("tensor", lambda e, kc_=kc, j_=j, w_=w, pb_=pb: e.matmul(
                            pb_[:], Wtile(w_, kc_)[:, 128 * j_:128 * (j_ + 1)], XBc(kc_),
                            start=(kc_ == 0), stop=(kc_ == 15)),
                            inc="sT" if kc == 15 else None)
                    if j == 3:
                        wslot_free[w] = [ev_mm]
                    B.wait("scalar", ev_mm)
                    ev_si = B.emit("scalar", lambda e, h_=h, pb_=pb: e.activation(
                        USL(h_), pb_[:], AF.Silu), inc="sA")
                    psum_free[bi] = ev_si

        def phase_att():
            i = 0
            for tt in range(4):
                pump_hook()
                for h in range(NH):
                    a = i % 2
                    pb_s = PSB[a]
                    pb_av = PSB[2 + a]
                    MS = MSA if a == 0 else MSB
                    B.wait("tensor", psum_free[a])
                    ev_s = B.emit("tensor", lambda e, tt_=tt, h_=h, pb_=pb_s: e.matmul(
                        pb_[:, 0:128], KN(h_)[:, 128 * tt_:128 * (tt_ + 1)],
                        QN(h_)[:, 128 * tt_:128 * (tt_ + 1)], start=True, stop=True), inc="sT")
                    B.wait("vector", ev_s)
                    B.wait("vector", ms_free[a])
                    ev_ms = B.emit("vector", lambda e, pb_=pb_s, MS_=MS: e.tensor_tensor(
                        MS_[:], pb_[:, 0:128], MASK[:], ALU.mult), inc="sV")
                    psum_free[a] = ev_ms
                    B.wait("tensor", ev_ms)
                    B.wait("tensor", psum_free[2 + a])
                    ev_a = B.emit("tensor", lambda e, tt_=tt, h_=h, pb_=pb_av, MS_=MS: e.matmul(
                        pb_[:, 0:128], GVT(tt_)[:, 128 * h_:128 * (h_ + 1)], MS_[:],
                        start=True, stop=True), inc="sT")
                    ms_free[a] = ev_a
                    B.wait("vector", ev_a)
                    ev_u = B.emit("vector", lambda e, tt_=tt, h_=h, pb_=pb_av: e.tensor_tensor(
                        USL(h_)[:, 128 * tt_:128 * (tt_ + 1)], pb_[:, 0:128],
                        USL(h_)[:, 128 * tt_:128 * (tt_ + 1)], ALU.mult), inc="sV",
                        drain=(i == 63))
                    psum_free[2 + a] = ev_u
                    i += 1

        def phase_oproj():
            B.wait("gpsimd", brow_free[0])
            ev_b = dma("gpsimd", BROW[:], ob[:, :], inc="sC")
            B.wait("tensor", ev_b)
            ev_stats = None
            for s in range(4):
                w = load_wslice(cs(woT, 512 * s, 512), cast=True)
                for j in range(4):
                    och = 4 * s + j
                    bi = next_bank2()
                    pb = PSB[bi]
                    B.wait("tensor", psum_free[bi])
                    B.wait("tensor", wslot_ready[w])
                    for kc in range(16):
                        B.emit("tensor", lambda e, kc_=kc, j_=j, w_=w, pb_=pb: e.matmul(
                            pb_[:], Wtile(w_, kc_)[:, 128 * j_:128 * (j_ + 1)], USL(kc_),
                            start=(kc_ == 0), stop=False))
                    ev_mm = B.emit("tensor", lambda e, pb_=pb, och_=och: e.matmul(
                        pb_[:], BROW[0:1, 128 * och_:128 * (och_ + 1)], ONE_ROW_B,
                        start=False, stop=True), inc="sT")
                    if j == 3:
                        wslot_free[w] = [ev_mm]
                    B.wait("vector", ev_mm)
                    B.wait("vector", rbsq_free[0])
                    ev_r = B.emit("vector", lambda e, pb_=pb, och_=och: e.tensor_tensor(
                        Xc(och_), pb_[:], Xc(och_), ALU.add), inc="sV", drain=False)
                    psum_free[bi] = ev_r
                    B.wait("vector", ev_r)
                    B.emit("vector", lambda e, och_=och: e.tensor_copy(RB[:], Xc(och_)))
                    ev_sqb = B.emit("vector", lambda e, och_=och: e.tensor_tensor(
                        SQB[:], Xc(och_), Xc(och_), ALU.mult), inc="sV")
                    B.wait("tensor", ev_sqb)
                    B.wait("tensor", strips_free[0])
                    B.emit("tensor", lambda e, och_=och: e.matmul(
                        PSSUM[:], ONE_COL_B, RB[:], start=(och_ == 0), stop=(och_ == 15)))
                    ev_stats = B.emit("tensor", lambda e, och_=och: e.matmul(
                        PSSQ[:], ONE_COL_B, SQB[:], start=(och_ == 0), stop=(och_ == 15)), inc="sT")
                    rbsq_free[0] = ev_stats
            brow_free[0] = ev_mm
            return ev_stats

        def phase_ln(ev_stats):
            B.wait("vector", ev_stats)
            B.emit("vector", lambda e: e.tensor_scalar(
                STRr(0), PSSUM[0:1, :], 1.0 / D, None, ALU.mult))
            ev_strips = B.emit("vector", lambda e: e.tensor_scalar(
                STRr(1), PSSQ[0:1, :], 1.0 / D, None, ALU.mult), inc="sV")
            strips_free[0] = ev_strips
            B.emit("vector", lambda e: e.tensor_tensor(STRr(2), STRr(0), STRr(0), ALU.mult), inc="sV")
            B.emit("vector", lambda e: e.tensor_tensor(STRr(2), STRr(1), STRr(2), ALU.subtract), inc="sV")
            B.emit("vector", lambda e: e.tensor_scalar(STRr(3), STRr(2), LN_EPS, None, ALU.add), inc="sV")
            B.emit("vector", lambda e: e.reciprocal(STRr(4), STRr(3)))
            ev = B.emit("vector", lambda e: e.drain(), inc="sV")
            B.wait("scalar", ev)
            ev = B.emit("scalar", lambda e: e.activation(STRr(5), STRr(4), AF.Sqrt), inc="sA")
            B.wait("tensor", ev)
            B.wait("tensor", ev_strips)
            B.wait("tensor", trf_free[0])
            ev = B.emit("tensor", lambda e: e.matmul(
                PSTRF[:], ONESF[0:1, :], STRr(0), start=True, stop=True), inc="sT")
            B.wait("vector", ev)
            ev = B.emit("vector", lambda e: e.tensor_copy(SBC[:], PSTRF[:]), inc="sV")
            B.wait("tensor", ev)
            ev = B.emit("tensor", lambda e: e.matmul(
                PSTRF[:], ONESF[0:1, :], STRr(5), start=True, stop=True), inc="sT")
            B.wait("vector", ev)
            ev = B.emit("vector", lambda e: e.tensor_copy(IVBC[:], PSTRF[:]), inc="sV")
            trf_free[0] = ev
            ev_ap = None
            for kc0 in range(0, 16, 2):
                for i in range(2):
                    remit2(lambda e, kc_=kc0 + i, G_=(GT, GT2)[i]: e.tensor_tensor(
                        G_[:], Xc(kc_), SBC[:], ALU.subtract))
                for i in range(2):
                    last = kc0 + i == 15
                    ev_ap = remit2(lambda e, kc_=kc0 + i, G_=(GT, GT2)[i]: e.tensor_tensor(
                        Xc(kc_), G_[:], IVBC[:], ALU.mult), drain=last)
            return ev_ap

        def make_scale_bcasts(ev_small):
            B.wait("tensor", ev_small)
            B.wait("tensor", trf_free[0])
            for half, dstc in ((0, 4), (1, 5)):
                ev = None
                for wdx in range(4):
                    ev = B.emit("tensor", lambda e, w_=wdx, h_=half: e.transpose(
                        PSTRF[0:1, 128 * w_:128 * (w_ + 1)],
                        SMALL[:, 16 + 4 * h_ + w_:17 + 4 * h_ + w_], IDF[:]), inc="sT")
                B.wait("vector", ev)
                ev = B.emit("vector", lambda e, d_=dstc: e.tensor_copy(
                    STRr(d_), PSTRF[0:1, 0:512]), inc="sV")
                B.wait("tensor", ev)
            ev = B.emit("tensor", lambda e: e.matmul(
                PSTRF[:], ONESF[0:1, :], STRr(4), start=True, stop=True), inc="sT")
            B.wait("vector", ev)
            ev = B.emit("vector", lambda e: e.tensor_copy(SBC[:], PSTRF[:]), inc="sV")
            B.wait("tensor", ev)
            ev = B.emit("tensor", lambda e: e.matmul(
                PSTRF[:], ONESF[0:1, :], STRr(5), start=True, stop=True), inc="sT")
            B.wait("vector", ev)
            ev = B.emit("vector", lambda e: e.tensor_copy(IVBC[:], PSTRF[:]), inc="sV")
            trf_free[0] = ev

        LSET = ((UT, VT, GT), (UT2, VT2, GT2))

        lad_prev = [None, None]

        def ladder_pair(jobs, flush=False):
            n = len(jobs)
            ev_w = [None] * n
            ev_sg = [None] * n
            for i in range(n):
                B.wait("vector", lad_prev[i])
                ev_w[i] = B.emit("vector", lambda e, s_=jobs[i][0], U_=LSET[i][0]: e.tensor_tensor(
                    U_[:], s_, SBC[:], ALU.mult), inc="sV", drain=False)
            for li, c in enumerate(LADDER):
                s = 1.0 if li % 2 == 0 else -1.0
                for i in range(n):
                    wsrc = LSET[i][0] if li == 0 else LSET[i][1]
                    B.wait("scalar", ev_w[i])
                    ev_sg[i] = B.emit("scalar", lambda e, w_=wsrc, G_=LSET[i][2], s2_=s: e.activation(
                        G_[:], w_[:], AF.Sign, bias=HALFP[0:128, 0:1], scale=s2_), inc="sA")
                for i in range(n):
                    wsrc = LSET[i][0] if li == 0 else LSET[i][1]
                    B.wait("vector", ev_sg[i])
                    ev_w[i] = B.emit("vector", lambda e, w_=wsrc, V_=LSET[i][1], G_=LSET[i][2],
                                     k_=s * c / 2: e.scalar_tensor_tensor(
                        V_[:], G_[:], k_, w_[:], ALU.mult, ALU.subtract), inc="sV", drain=False)
            evt = [None] * n
            for i in range(n):
                B.wait("vector", ev_w[i])
                evt[i] = B.emit("vector", lambda e, U_=LSET[i][0], V_=LSET[i][1],
                                G_=LSET[i][2]: e.tensor_tensor(
                    G_[:], U_[:], V_[:], ALU.subtract), inc="sV", drain=False)
            ev = None
            for i in range(n):
                B.wait("vector", evt[i])
                ev = B.emit("vector", lambda e, d_=jobs[i][1], G_=LSET[i][2]: e.scalar_tensor_tensor(
                    d_, G_[:], -0.5, IVBC[:], ALU.add, ALU.mult), inc="sV",
                    drain=(flush and i == n - 1))
                lad_prev[i] = ev
            return ev

        def scale_math():
            B.emit("vector", lambda e: e.tensor_scalar(SMALL[:, 4:8], SMALL[:, 0:4], Q_EPS, None, ALU.max))
            B.emit("vector", lambda e: e.drain())
            B.emit("vector", lambda e: e.reciprocal(SMALL[:, 8:12], SMALL[:, 4:8]))
            B.emit("vector", lambda e: e.drain())
            B.emit("vector", lambda e: e.tensor_scalar(SMALL[:, 16:20], SMALL[:, 8:12], 127.0, None, ALU.mult))
            return B.emit("vector", lambda e: e.tensor_scalar(
                SMALL[:, 20:24], SMALL[:, 4:8], 1.0 / 127.0, None, ALU.mult), inc="sV")

        def phase_qx():
            remit2(lambda e: e.tensor_tensor(CM[:], Xc(0), Xc(0), ALU.max))
            remit2(lambda e: e.tensor_scalar(SBC[:], Xc(0), -1.0, None, ALU.mult))
            for kc in range(1, 16):
                remit2(lambda e, kc_=kc: e.tensor_tensor(CM[:], Xc(kc_), CM[:], ALU.max))
                evb = remit2(lambda e, kc_=kc: e.scalar_tensor_tensor(
                    SBC[:], Xc(kc_), -1.0, SBC[:], ALU.mult, ALU.max))
            B.wait("vector", evb)
            ev_cm = B.emit("vector", lambda e: e.tensor_tensor(
                CM[:], CM[:], SBC[:], ALU.max), inc="sV")
            B.wait("tensor", ev_cm)
            B.wait("tensor", trf_free[0])
            ev_t = None
            for wdx in range(4):
                ev_t = B.emit("tensor", lambda e, w_=wdx: e.transpose(
                    PSTRF[:, 128 * w_:128 * (w_ + 1)], CM[:, 128 * w_:128 * (w_ + 1)],
                    IDF[:]), inc="sT")
            B.wait("vector", ev_t)
            for wdx in range(4):
                B.emit("vector", lambda e, w_=wdx: e.tensor_reduce(
                    SMALL[:, w_:w_ + 1], PSTRF[:, 128 * w_:128 * (w_ + 1)], AX.X, ALU.max))
            ev = B.emit("vector", lambda e: e.drain(), inc="sV")
            trf_free[0] = ev
            ev_sm = scale_math()
            make_scale_bcasts(ev_sm)
            ev = None
            for kc in range(0, 16, 2):
                ev = ladder_pair([(Xc(kc), X2Qc(kc)), (Xc(kc + 1), X2Qc(kc + 1))],
                                 flush=(kc == 14))
            return ev

        def wq_halves(wt_r, c0, k0_base, scratch_r, rms_off, w):
            B.wait("vector", rmsc_ev[rms_off // 4])
            ev_q = None
            for qtr in range(4):
                B.wait("sync", wf_free[0])
                B.wait("sync", wf2_free[0])
                ev_wf = dma("sync", WF[:].rearrange("p (k n) -> p k n", n=512),
                            wt_r[:, k0_base + 4 * qtr:k0_base + 4 * qtr + 4, c0:c0 + 512],
                            inc="sF")
                B.wait("vector", ev_wf)
                if qtr == 0:
                    waitall("vector", wslot_free[w])
                for cp in (0, 2):
                    for i in range(2):
                        remit2(lambda e, c_=cp + i, r_=rms_off, U_=LSET[i][0]: e.tensor_scalar(
                            U_[:], WF[:, 512 * c_:512 * (c_ + 1)],
                            RMSC[:, r_:r_ + 1], RMSC[:, r_ + 2:r_ + 3], ALU.is_ge, ALU.mult))
                    for i in range(2):
                        remit2(lambda e, c_=cp + i, r_=rms_off, V_=LSET[i][1]: e.tensor_scalar(
                            V_[:], WF[:, 512 * c_:512 * (c_ + 1)],
                            RMSC[:, r_ + 1:r_ + 2], RMSC[:, r_ + 3:r_ + 4], ALU.is_le, ALU.mult))
                    for i in range(2):
                        last = qtr == 3 and cp == 2 and i == 1
                        ev_q = remit2(lambda e, w_=w, kcw_=4 * qtr + cp + i,
                                      U_=LSET[i][0], V_=LSET[i][1]: e.tensor_tensor(
                            Wtile(w_, kcw_), U_[:], V_[:], ALU.add), drain=last)
                wf_free[0] = ev_q
            wslot_ready[w] = ev_q
            B.wait("sync", ev_q)
            ev_s = dma("sync", scratch_r[:, k0_base:k0_base + 16, c0:c0 + 512],
                       WSLOT(w).rearrange("p (k n) -> p k n", n=512), inc="sS" + str(w))
            scratch_ev[w] = ev_s

        w1T_r = w1T.rearrange("(k p) n -> p k n", p=128)
        w2T_r = w2T.rearrange("(k p) n -> p k n", p=128)
        wq1S_r = wq1S.rearrange("(k p) n -> p k n", p=128)
        wq2S_r = wq2S.rearrange("(k p) n -> p k n", p=128)

        def ffn1(p):
            ev_h = None
            for sl in range(16):
                if p == 0:
                    w = wslot_next[0]
                    wslot_next[0] = 1 - w
                    wq_halves(w1T_r, 512 * sl, 0, wq1S_r, 0, w)
                else:
                    w = load_wslice(wq1S_r[:, :, 512 * sl:512 * (sl + 1)], cast=False)
                for j in range(4):
                    hc = 4 * sl + j
                    bi = next_bank2()
                    pb = PSB[bi]
                    B.wait("tensor", psum_free[bi])
                    B.wait("tensor", wslot_ready[w])
                    ev_mm = None
                    for kc in range(16):
                        ev_mm = B.emit("tensor", lambda e, kc_=kc, j_=j, w_=w, pb_=pb: e.matmul(
                            pb_[:], Wtile(w_, kc_)[:, 128 * j_:128 * (j_ + 1)], X2Qc(kc_),
                            start=(kc_ == 0), stop=(kc_ == 15)),
                            inc="sT" if kc == 15 else None)
                    if j == 3:
                        wslot_free[w] = [ev_mm] + ([scratch_ev[w]] if scratch_ev[w] else [])
                        scratch_ev[w] = None
                    B.wait("scalar", ev_mm)
                    ev_g = B.emit("scalar", lambda e, hc_=hc, pb_=pb: e.activation(
                        Hc(hc_), pb_[:], AF.Gelu), inc="sA")
                    psum_free[bi] = ev_g
                    B.wait("vector", ev_g)
                    if hc == 0:
                        remit2(lambda e: e.tensor_copy(CMHB[:], Hc(0)))
                        ev_h = remit2(lambda e: e.tensor_scalar(
                            CMHB2[:], Hc(0), -1.0, None, ALU.mult))
                    else:
                        remit2(lambda e, hc_=hc: e.tensor_tensor(
                            CMHB[:], Hc(hc_), CMHB[:], ALU.max))
                        ev_h = remit2(lambda e, hc_=hc: e.scalar_tensor_tensor(
                            CMHB2[:], Hc(hc_), -1.0, CMHB2[:], ALU.mult, ALU.max))
            return ev_h

        def phase_qh(ev_cmh):
            B.wait("vector", ev_cmh)
            ev_cm = B.emit("vector", lambda e: e.tensor_tensor(
                CMHB[:], CMHB[:], CMHB2[:], ALU.max), inc="sV")
            B.wait("tensor", ev_cm)
            B.wait("tensor", trb_free[0])
            ev_t = None
            for wdx in range(4):
                ev_t = B.emit("tensor", lambda e, w_=wdx: e.transpose(
                    PSTRB[:, 128 * w_:128 * (w_ + 1)], CMHB[:, 128 * w_:128 * (w_ + 1)],
                    IDB[:]), inc="sT")
            B.wait("vector", ev_t)
            for wdx in range(4):
                B.emit("vector", lambda e, w_=wdx: e.tensor_reduce(
                    SMALL[:, w_:w_ + 1], PSTRB[:, 128 * w_:128 * (w_ + 1)], AX.X, ALU.max))
            ev = B.emit("vector", lambda e: e.drain(), inc="sV")
            trb_free[0] = ev
            ev_sm = scale_math()
            make_scale_bcasts(ev_sm)
            ev = None
            for hc in range(0, NHC, 2):
                ev = ladder_pair([(Hc(hc), Hc(hc)), (Hc(hc + 1), Hc(hc + 1))],
                                 flush=(hc == NHC - 2))
            return ev

        def ffn2(p):
            ev_stats = None
            for og in range(4):
                ev_mm = None
                for qt in range(4):
                    if p == 0:
                        w = wslot_next[0]
                        wslot_next[0] = 1 - w
                        wq_halves(w2T_r, 512 * og, 16 * qt, wq2S_r, 4, w)
                    else:
                        w = load_wslice(
                            wq2S_r[:, 16 * qt:16 * (qt + 1), 512 * og:512 * (og + 1)],
                            cast=False)
                    if qt == 0:
                        for bi in range(4):
                            B.wait("tensor", psum_free[bi])
                    B.wait("tensor", wslot_ready[w])
                    for kc in range(16):
                        for j in range(4):
                            last = (kc == 15 and j == 3)
                            ev_mm = B.emit("tensor", lambda e, kc_=kc, j_=j, qt_=qt, w_=w: e.matmul(
                                PSB[j_][:], Wtile(w_, kc_)[:, 128 * j_:128 * (j_ + 1)],
                                Hc(16 * qt_ + kc_),
                                start=(qt_ == 0 and kc_ == 0), stop=(qt_ == 3 and kc_ == 15)),
                                inc="sT" if last else None)
                    wslot_free[w] = [ev_mm] + ([scratch_ev[w]] if scratch_ev[w] else [])
                    scratch_ev[w] = None
                B.wait("vector", ev_mm)
                for j in range(4):
                    och = 4 * og + j
                    B.wait("vector", rbsq_free[0])
                    ev_r = B.emit("vector", lambda e, j_=j, och_=och: e.tensor_tensor(
                        Xc(och_), PSB[j_][:], Xc(och_), ALU.add), inc="sV", drain=False)
                    psum_free[j] = ev_r
                    B.wait("vector", ev_r)
                    B.emit("vector", lambda e, och_=och: e.tensor_copy(RB[:], Xc(och_)))
                    ev_sqb = B.emit("vector", lambda e, och_=och: e.tensor_tensor(
                        SQB[:], Xc(och_), Xc(och_), ALU.mult), inc="sV")
                    B.wait("tensor", ev_sqb)
                    B.wait("tensor", strips_free[0])
                    first = (og == 0 and j == 0)
                    lastt = (og == 3 and j == 3)
                    B.emit("tensor", lambda e, f_=first, l_=lastt: e.matmul(
                        PSSUM[:], ONE_COL_B, RB[:], start=f_, stop=l_))
                    ev_stats = B.emit("tensor", lambda e, f_=first, l_=lastt: e.matmul(
                        PSSQ[:], ONE_COL_B, SQB[:], start=f_, stop=l_), inc="sT")
                    rbsq_free[0] = ev_stats
            return ev_stats

        def phase_out(p, ev_ln2):
            B.wait("sync", ev_ln2)
            return dma("sync", outT.rearrange("(k p) n -> p k n", p=128)[:, :, TP * p:TP * (p + 1)],
                       X.rearrange("p (k n) -> p k n", n=512), inc="sO")

        out_ev = None
        for p in range(npass):
            hook["fn"] = (lambda: prepass_pump(8)) if p == 0 else None
            xb_ev = phase_x_load(p, out_ev)
            pump_hook()
            pump_hook()
            B.barrier()
            if p == 0:
                _dbg("xb", XBfull)
            phase_beta(xb_ev)
            pump_hook()
            if p == 0 and DEBUG:
                B.barrier()
                _dbg("beta", BETA[:])
            phase_qk(wqT, qb, QNr)
            if p == 0 and DEBUG:
                B.barrier()
                _dbg("qn", BIGA[:, 0:8192])
            phase_qk(wkT, kb, KNr)
            phase_v()
            if p == 0 and DEBUG:
                B.barrier()
                _dbg("kn", BIGA[:, 8192:16384])
                _dbg("gv", BIGA[:, 16384:24576])
            phase_og()
            B.barrier()
            if p == 0:
                _dbg("gate", BIGA[:, 24576:32768])
            phase_att()
            B.barrier()
            if p == 0:
                _dbg("u", BIGA[:, 24576:32768])
            ev_stats = phase_oproj()
            B.barrier()
            if p == 0:
                _dbg("r1", X[:])
            phase_ln(ev_stats)
            pump_hook()
            B.barrier()
            if p == 0:
                _dbg("x2", X[:])
            phase_qx()
            if p == 0:
                prepass_pump(256)
            B.barrier()
            if p == 0:
                _dbg("x2q", BIGA[:, 32768:40960])
                _dbg("sbc", SBC[:])
                _dbg("ivbc", IVBC[:])
            ev_cmh = ffn1(p)
            B.barrier()
            if p == 0:
                _dbg("h", BIGA[:, 0:32768])
                _dbg("cmhb", CMHB[:])
            phase_qh(ev_cmh)
            B.barrier()
            if p == 0:
                _dbg("hq", BIGA[:, 0:32768])
            ev_stats2 = ffn2(p)
            B.barrier()
            if p == 0:
                _dbg("r2", X[:])
            if p == 0 and "wq2s" in DEBUG:
                t1 = nc.dram_tensor("dbg_wq1s", [D, DF], BF16, kind="ExternalOutput").ap()
                t2 = nc.dram_tensor("dbg_wq2s", [DF, D], BF16, kind="ExternalOutput").ap()
                B.barrier()
                ev1 = dma("sync", t1[:, :], wq1S[:, :], inc="sO")
                ev2 = dma("sync", t2[:, :], wq2S[:, :], inc="sO")
                for eng in ENGINES:
                    B.wait(eng, ev2)
            ev_ln2 = phase_ln(ev_stats2)
            out_ev = phase_out(p, ev_ln2)
            B.barrier()
        for name in list(B.cnt):
            if name.startswith(("sO", "sS")):
                B.wait("sync", B.now(name))

        with contextlib.ExitStack() as stack:
            sems = {name: stack.enter_context(nc.semaphore("sem_" + name)) for name in B.cnt}
            with nc.Block() as block:
                B.replay(block, sems)

    return nc


def _host_prep(inputs):
    f32 = np.float32
    x = np.asarray(inputs["L0_attn_hidden_states"], f32)[0]
    xT = np.ascontiguousarray(x.T)
    com = {
        "wqT": np.ascontiguousarray(np.asarray(inputs["L0_attn_q_w"], f32).T),
        "wkT": np.ascontiguousarray(np.asarray(inputs["L0_attn_k_w"], f32).T),
        "wvT": np.ascontiguousarray(np.asarray(inputs["L0_attn_v_w"], f32).T),
        "wogT": np.ascontiguousarray(np.asarray(inputs["L0_attn_og_w"], f32).T),
        "woT": np.ascontiguousarray(np.asarray(inputs["L0_attn_o_w"], f32).T),
        "w1T": np.ascontiguousarray(np.asarray(inputs["L0_ffn_w1"], f32).T),
        "w2T": np.ascontiguousarray(np.asarray(inputs["L0_ffn_w2"], f32).T),
        "bwT": np.ascontiguousarray(np.asarray(inputs["L0_attn_b_w"], f32).T),
        "qb": np.ascontiguousarray(np.asarray(inputs["L0_attn_q_b"], f32)[None, :]),
        "kb": np.ascontiguousarray(np.asarray(inputs["L0_attn_k_b"], f32)[None, :]),
        "vb": np.ascontiguousarray(np.asarray(inputs["L0_attn_v_b"], f32)[None, :]),
        "ob": np.ascontiguousarray(np.asarray(inputs["L0_attn_o_b"], f32)[None, :]),
        "bb": np.ascontiguousarray(np.asarray(inputs["L0_attn_b_b"], f32)[None, :]),
    }
    com["idb"] = np.eye(128, dtype=ml_dtypes.bfloat16)
    com["idf"] = np.eye(128, dtype=f32)
    pos = np.arange(64, dtype=f32)
    dist = np.clip(pos[None, :] - pos[:, None], 0.0, None)
    decay = np.exp(-0.1 * dist).astype(f32)
    causal = pos[None, :] <= pos[:, None]
    m64 = np.where(causal, decay, 0.0).astype(f32)
    m128 = np.zeros((128, 128), f32)
    m128[0:64, 0:64] = m64.T
    m128[64:128, 64:128] = m64.T
    com["maskf"] = m128
    com["onesb"] = np.ones((128, 512), dtype=ml_dtypes.bfloat16)
    com["onesf"] = np.ones((8, 128), dtype=f32)
    com["onecf"] = np.ones((128, 1), dtype=f32)
    return xT, com


def kernel(**inputs):
    xT, com = _host_prep(inputs)
    L = xT.shape[1]
    tpc = L // NCORES
    nc = build_program(tpc)
    in_maps = []
    for c in range(NCORES):
        m = dict(com)
        m["xT"] = np.ascontiguousarray(xT[:, c * tpc:(c + 1) * tpc])
        in_maps.append(m)
    res = run_bass_kernel_spmd(nc, in_maps, core_ids=list(range(NCORES)))
    parts = [np.asarray(r["outT"]).T for r in res.results]
    out = np.concatenate(parts, axis=0)[None, :, :]
    return np.ascontiguousarray(out, dtype=np.float32)
